# version 13
# baseline (speedup 1.0000x reference)
import numpy as np
from contextlib import ExitStack

import concourse.bass as bass
import concourse.mybir as mybir
from concourse.bass_utils import run_bass_kernel_spmd

F32 = mybir.dt.float32
F16 = mybir.dt.float16
AF = mybir.ActivationFunctionType
ALU = mybir.AluOpType
AX = mybir.AxisListType

B, S, D, NH, HD, FF = 2, 4096, 2048, 16, 128, 8192
TC = 1024
NHC = 4
DC = D // 128
FC = FF // 128
NEG = -1.0e30
MAGIC = 1536.0
EPS = 1e-5
INV_SQRT_HD = float(1.0 / np.sqrt(128.0))
W_CNT = float(D * FF)

ENGINES = ("tensor", "vector", "scalar", "gpsimd", "sync")


class K:

    def __init__(self, nc, st):
        self.nc = nc
        self.blk = None
        self.buf = {e: [] for e in ENGINES}
        self.sem = {}
        self.cnt = {}
        for e in ENGINES:
            for kind in ("c", "d", "t"):
                self.sem[(kind, e)] = st.enter_context(nc.semaphore(f"{kind}_{e}"))
                self.cnt[(kind, e)] = 0
        self.sem["cc"] = st.enter_context(nc.semaphore("cc"))
        self.cnt["cc"] = 0

    def op(self, eng, fn, inc=True):
        key = ("c", eng)
        t = None
        if inc:
            self.cnt[key] += 1
            t = (key, self.cnt[key])
        self.buf[eng].append(("op", fn, key if inc else None, 1))
        return t

    def dma(self, eng, fn, kind="d"):
        key = (kind, eng)
        self.cnt[key] += 16
        t = (key, self.cnt[key])
        self.buf[eng].append(("op", fn, key, 16))
        return t

    def tdma(self, eng, fn):
        return self.dma(eng, fn, kind="t")

    def cc(self, fn):
        self.cnt["cc"] += 1
        t = ("cc", self.cnt["cc"])
        self.buf["gpsimd"].append(("op", fn, "cc", 1))
        return t

    def chain(self, eng, fns):
        for f in fns[:-1]:
            self.buf[eng].append(("op", f, None, 1))
        return self.op(eng, fns[-1])

    def wait(self, eng, t):
        if t is None:
            return
        self.buf[eng].append(("wait", t))

    def barrier(self):
        for e in ENGINES:
            for key, v in self.cnt.items():
                if v > 0 and key != ("c", e):
                    self.buf[e].append(("wait", (key, v)))

    def wait_all(self, eng, ts):
        best = {}
        for t in ts:
            if t is None:
                continue
            key, v = t
            if key not in best or v > best[key]:
                best[key] = v
        for key, v in best.items():
            self.buf[eng].append(("wait", (key, v)))

    def flush(self):
        for e in ENGINES:
            items = self.buf[e]
            if not items:
                continue
            self.buf[e] = []
            sems = self.sem

            def body(eng, items=items):
                for it in items:
                    if it[0] == "op":
                        _, fn, key, amt = it
                        instr = fn(eng)
                        if key is not None:
                            instr.then_inc(sems[key], amt)
                    else:
                        key, v = it[1]
                        eng.wait_ge(sems[key], v)

            getattr(self.blk, e)(body)


def build_program(taps=()):
    nc = bass.Bass("TRN2", target_bir_lowering=False)

    x_full = nc.dram_tensor("x_full", [S, D], F32, kind="ExternalInput").ap()
    x_own = nc.dram_tensor("x_own", [TC, D], F32, kind="ExternalInput").ap()
    wqT = nc.dram_tensor("wqT", [D, 512], F32, kind="ExternalInput").ap()
    wkT = nc.dram_tensor("wkT", [D, 512], F32, kind="ExternalInput").ap()
    wvT = nc.dram_tensor("wvT", [D, 512], F32, kind="ExternalInput").ap()
    woTs = nc.dram_tensor("woTs", [512, D], F32, kind="ExternalInput").ap()
    cfc = nc.dram_tensor("cfc", [D, FF], F32, kind="ExternalInput").ap()
    cpT = nc.dram_tensor("cpT", [FF, D], F32, kind="ExternalInput").ap()
    cfc_sl = nc.dram_tensor("cfc_sl", [256, FF], F32, kind="ExternalInput").ap()
    cpT_sl = nc.dram_tensor("cpT_sl", [1024, D], F32, kind="ExternalInput").ap()
    identD = nc.dram_tensor("identD", [128, 128], F32, kind="ExternalInput").ap()
    maskD = nc.dram_tensor("maskD", [128, 128], F32, kind="ExternalInput").ap()
    out = nc.dram_tensor("out", [TC, D], F32, kind="ExternalOutput").ap()

    asum_in = nc.dram_tensor("asum_in", [128, 2], F32, kind="Internal",
                             addr_space="Local").ap()
    asum_out = nc.dram_tensor("asum_out", [128, 2], F32, kind="Internal",
                              addr_space="Local").ap()
    rs_in_c = [nc.dram_tensor(f"rs_in{kk}", [TC, D], F16, kind="Internal",
                              addr_space="Local").ap() for kk in range(4)]
    rs_out_c = [nc.dram_tensor(f"rs_out{kk}", [256, D], F16, kind="Internal",
                               addr_space="Local").ap() for kk in range(4)]
    t1c = nc.dram_tensor("t1c", [D, FF], F16, kind="Internal",
                         addr_space="Local").ap()
    t2c = nc.dram_tensor("t2c", [FF, D], F16, kind="Internal",
                         addr_space="Local").ap()
    attn_d = nc.dram_tensor("attn_d", [NHC * HD, S], F16, kind="Internal",
                            addr_space="Local").ap()

    tap_ap = {}

    def mk_tap(name, shape, dtype):
        if name in taps:
            tap_ap[name] = nc.dram_tensor("tap_" + name, shape, dtype,
                                          kind="ExternalOutput").ap()
            return tap_ap[name]
        return None

    tp_yfm = mk_tap("yfm", [128, DC, S], F32)
    tp_q = mk_tap("q", [128, NHC, S], F32)
    tp_k = mk_tap("k", [128, NHC, S], F32)
    tp_v = mk_tap("v", [128, S // 128, 512], F32)
    tp_attn = mk_tap("attn", [NHC * HD, S], F32)
    tp_rs = mk_tap("rs", [128, 4], F32)
    tp_t1 = mk_tap("t1", [D, FF], F32)
    tp_t2 = mk_tap("t2", [FF, D], F32)
    tp_x1 = mk_tap("x1", [128, 8, D], F32)
    tp_z = mk_tap("z", [128, DC, TC], F32)
    tp_h = mk_tap("h", [128, FC, 512], F32)
    tp_bp = mk_tap("bp", [128, 8], F32)
    tp_e_z = mk_tap("e_z", [128, DC, TC], F32)
    tp_e_x1 = mk_tap("e_x1", [128, 8, D], F32)
    tp_e_h = mk_tap("e_h", [128, FC, 512], F32)
    tp_e_bp = mk_tap("e_bp", [128, 8], F32)
    tp_e_bbc = mk_tap("e_bbc", [128, 512], F32)
    tp_rsi = mk_tap("rsi", [4, TC, D], F32)
    tp_rso = mk_tap("rso", [4, 256, D], F32)

    with ExitStack() as st:
        id32 = st.enter_context(nc.sbuf_tensor("id32", [128, 128], F32))
        ones_row = st.enter_context(nc.sbuf_tensor("ones_row", [1, 128], F32))
        eps_col = st.enter_context(nc.sbuf_tensor("eps_col", [128, 1], F32))
        magic_col = st.enter_context(nc.sbuf_tensor("magic_col", [128, 1], F32))
        rsbc = st.enter_context(nc.sbuf_tensor("rsbc", [128, 4], F32))
        stat = st.enter_context(nc.sbuf_tensor("stat", [128, 64], F32))
        bnst = st.enter_context(nc.sbuf_tensor("bnst", [128, 24], F32))
        srow = st.enter_context(nc.sbuf_tensor("srow", [1, 4], F32))

        k = K(nc, st)
        blk = st.enter_context(nc.Block())
        k.blk = blk

        t_id = k.dma("sync", lambda e: e.dma_start(out=id32[:], in_=identD[:]))
        t_onesr = k.op("vector", lambda e: e.memset(ones_row[:], 1.0))
        t_eps = k.op("vector", lambda e: e.memset(eps_col[:], EPS))
        t_mgc = k.op("vector", lambda e: e.memset(magic_col[:], MAGIC))

        sS = ExitStack()
        sS.__enter__()
        ones_col = sS.enter_context(nc.sbuf_tensor("ones_col", [128, 1], F32))
        awb = sS.enter_context(nc.sbuf_tensor("awb", [128, 2, 2048], F32))
        adisc = sS.enter_context(nc.sbuf_tensor("adisc", [128, 2048], F16))
        acol = sS.enter_context(nc.sbuf_tensor("acol", [128, 16], F32))
        prt = sS.enter_context(nc.sbuf_tensor("prt", [128, 2], F32))
        ps_s = sS.enter_context(nc.psum_tensor("ps_s", [128, 8], F32))
        t_onesc = k.op("vector", lambda e: e.memset(ones_col[:], 1.0))

        tiles = []
        for i in range(2):
            for cb in range(4):
                tiles.append(cfc_sl[i * 128:(i + 1) * 128,
                                    cb * 2048:(cb + 1) * 2048])
        for i in range(8):
            tiles.append(cpT_sl[i * 128:(i + 1) * 128, :])
        prev_act = [None, None]
        for idx, src in enumerate(tiles):
            pp = idx % 2
            t_ld = k.dma("scalar", lambda e, src=src, pp=pp: e.dma_start(
                out=awb[:, pp, :], in_=src))
            k.wait("scalar", t_ld)
            t_a = k.op("scalar", lambda e, pp=pp, idx=idx: e.activation(
                adisc[:], awb[:, pp, :], AF.Abs,
                accum_out=acol[:, idx:idx + 1]))
            prev_act[pp] = t_a
        k.wait_all("vector", prev_act)
        t_p1 = k.op("vector", lambda e: e.reduce_sum(
            prt[:, 0:1], acol[:, 0:8], axis=AX.X))
        t_p2 = k.op("vector", lambda e: e.reduce_sum(
            prt[:, 1:2], acol[:, 8:16], axis=AX.X))
        k.wait_all("sync", [t_p1, t_p2])
        t_st = k.dma("sync", lambda e: e.dma_start(out=asum_in[:], in_=prt[:]))
        k.wait("gpsimd", t_st)
        t_cc0 = k.cc(lambda e: e.collective_compute(
            "AllReduce", ALU.add, replica_groups=[[0, 1, 2, 3, 4, 5, 6, 7]],
            ins=[asum_in.opt()], outs=[asum_out.opt()]))
        t_cc = k.cc(lambda e: e.collective_compute(
            "AllReduce", ALU.add, replica_groups=[[0, 1, 2, 3, 4, 5, 6, 7]],
            ins=[asum_in.opt()], outs=[asum_out.opt()]))
        k.wait("sync", t_cc)
        t_ld2 = k.dma("sync", lambda e: e.dma_start(out=prt[:], in_=asum_out[:]))
        k.wait("tensor", t_ld2)
        k.wait("tensor", t_onesc)
        t_tot = k.op("tensor", lambda e: e.matmul(
            ps_s[0:1, 0:2], ones_col[:], prt[:], start=True, stop=True))
        k.wait("vector", t_tot)
        t_s = k.op("vector", lambda e: e.tensor_scalar_mul(
            srow[0:1, 0:2], ps_s[0:1, 0:2], 1.0 / W_CNT))
        k.wait("vector", t_s)
        t_r = k.op("vector", lambda e: e.reciprocal(
            srow[0:1, 2:4], srow[0:1, 0:2]))
        k.wait_all("tensor", [t_s, t_r, t_onesr])
        t_bc = k.op("tensor", lambda e: e.matmul(
            ps_s[:, 4:8], ones_row[:], srow[:], start=True, stop=True))
        k.wait("vector", t_bc)
        t_rsbc = k.op("vector", lambda e: e.tensor_copy(rsbc[:], ps_s[:, 4:8]))
        if tp_rs is not None:
            k.wait("sync", t_rsbc)
            k.dma("sync", lambda e: e.dma_start(out=tp_rs, in_=rsbc[:]))
        k.flush()
        sS.__exit__(None, None, None)

        sQ = ExitStack()
        sQ.__enter__()
        qin = sQ.enter_context(nc.sbuf_tensor("qin", [128, 2, 512], F32))
        qmid = sQ.enter_context(nc.sbuf_tensor("qmid", [128, 2, 512], F32))
        qot = sQ.enter_context(nc.sbuf_tensor("qot", [128, 2, 512], F16))
        qstate = {"t1_done": None, "t2_done": None}

        def quant_gen():
            jobs = []
            for dc in range(DC):
                for cb in range(16):
                    jobs.append((cfc[dc * 128:(dc + 1) * 128,
                                     cb * 512:(cb + 1) * 512],
                                 t1c[dc * 128:(dc + 1) * 128,
                                     cb * 512:(cb + 1) * 512], 0))
            for hc in range(FC):
                for cb in range(4):
                    jobs.append((cpT[hc * 128:(hc + 1) * 128,
                                     cb * 512:(cb + 1) * 512],
                                 t2c[hc * 128:(hc + 1) * 128,
                                     cb * 512:(cb + 1) * 512], 1))
            prev_p1 = [None, None]
            prev_p1b = [None, None]
            prev_stq = [None, None]
            n_done = 0
            for ji, (src, dst, wi) in enumerate(jobs):
                pp = ji % 2
                k.wait("scalar", prev_p1[pp])
                t_ld = k.dma("scalar", lambda e, src=src, pp=pp: e.dma_start(
                    out=qin[:, pp, :], in_=src))
                k.wait("scalar", t_ld)
                k.wait("scalar", prev_p1b[pp])
                if ji == 0:
                    k.wait("scalar", t_rsbc)
                    k.wait("scalar", t_mgc)
                t_p1 = k.op("scalar", lambda e, pp=pp, wi=wi: e.activation(
                    qmid[:, pp, :], qin[:, pp, :], AF.Identity,
                    bias=magic_col[:], scale=rsbc[:, 2 + wi:3 + wi]))
                prev_p1[pp] = t_p1
                k.wait("vector", t_p1)
                k.wait("vector", prev_stq[pp])
                t_p2 = k.op("vector", lambda e, pp=pp: e.tensor_scalar(
                    qot[:, pp, :], qmid[:, pp, :], MAGIC - 1.0, MAGIC + 1.0,
                    ALU.max, op1=ALU.min))
                k.wait("vector", t_p2)
                t_p3 = k.op("vector", lambda e, pp=pp: e.tensor_scalar_sub(
                    qot[:, pp, :], qot[:, pp, :], MAGIC))
                prev_p1b[pp] = t_p3
                k.wait("gpsimd", t_p3)
                t_stq = k.dma("gpsimd", lambda e, dst=dst, pp=pp: e.dma_start(
                    out=dst, in_=qot[:, pp, :]))
                prev_stq[pp] = t_stq
                n_done += 1
                if n_done == 256:
                    qstate["t1_done"] = t_stq
                if n_done == 512:
                    qstate["t2_done"] = t_stq
                yield

        qg = quant_gen()

        def pump(n):
            for _ in range(n):
                if next(qg, "END") == "END":
                    break

        sAB = ExitStack()
        sAB.__enter__()
        mask = sAB.enter_context(nc.sbuf_tensor("mask", [128, 128], F32))
        q_fm = sAB.enter_context(nc.sbuf_tensor("q_fm", [128, NHC, S], F16))
        k_fm = sAB.enter_context(nc.sbuf_tensor("k_fm", [128, NHC, S], F16))
        v_tm = sAB.enter_context(nc.sbuf_tensor("v_tm", [128, S // 128, 512], F16))
        t_mask = k.dma("sync", lambda e: e.dma_start(out=mask[:], in_=maskD[:]))

        sA = ExitStack()
        with sA:
            xt = sA.enter_context(nc.sbuf_tensor("xt", [128, 2, D], F32))
            yt = sA.enter_context(nc.sbuf_tensor("yt", [128, 2, D], F16))
            y_fm = sA.enter_context(nc.sbuf_tensor(
                "y_fm", [128, DC, 512], F16, is_transpose_dest=True))
            wq16 = sA.enter_context(nc.sbuf_tensor("wq16", [128, DC, 512], F16))
            wk16 = sA.enter_context(nc.sbuf_tensor("wk16", [128, DC, 512], F16))
            wv16 = sA.enter_context(nc.sbuf_tensor("wv16", [128, DC, 512], F16))
            ps_a = sA.enter_context(nc.psum_tensor("ps_a", [128, 4, 512], F32))

            t_wq = k.dma("gpsimd", lambda e: e.dma_start(
                out=wq16[:], in_=wqT.rearrange("(c p) n -> p c n", p=128)))
            t_wk = k.dma("gpsimd", lambda e: e.dma_start(
                out=wk16[:], in_=wkT.rearrange("(c p) n -> p c n", p=128)))
            t_wv = k.dma("gpsimd", lambda e: e.dma_start(
                out=wv16[:], in_=wvT.rearrange("(c p) n -> p c n", p=128)))

            k.wait("scalar", t_eps)
            prev_apply = [None, None]
            prev_tp = [None, None]
            last_qkv_mm = None
            pp_mm = 0
            prev_evac = [None, None, None, None]
            qkv_done = []

            for seg in range(8):
                tp_ticks = []
                for xi in range(4):
                    ti = seg * 4 + xi
                    b2 = ti % 2
                    k.wait("sync", prev_apply[b2])
                    t_x = k.dma("sync", lambda e, ti=ti, b2=b2: e.dma_start(
                        out=xt[:, b2, :], in_=x_full[ti * 128:(ti + 1) * 128, :]))
                    k.wait("vector", t_x)
                    fns = []
                    for q4 in range(4):
                        fns.append(lambda e, b2=b2, q4=q4: e.bn_stats(
                            bnst[:, q4 * 6:(q4 + 1) * 6],
                            xt[:, b2, q4 * 512:(q4 + 1) * 512]))
                    t_bn = k.chain("vector", fns)
                    k.wait("vector", t_bn)
                    t_agg = k.op("vector", lambda e: e.bn_aggr(stat[:, 0:2], bnst[:]))
                    k.wait("scalar", t_agg)
                    t_rstd = k.chain("scalar", [
                        lambda e: e.activation(stat[:, 2:3], stat[:, 1:2],
                                               AF.Ln, bias=eps_col[:]),
                        lambda e: e.activation(stat[:, 3:4], stat[:, 2:3],
                                               AF.Exp, scale=-0.5),
                    ])
                    k.wait("vector", t_rstd)
                    k.wait_all("vector", prev_apply)
                    t_mb0 = k.op("vector", lambda e: e.tensor_tensor(
                        stat[:, 16:17], stat[:, 0:1], stat[:, 3:4], ALU.mult))
                    k.wait("vector", t_mb0)
                    t_mb = k.op("vector", lambda e: e.tensor_scalar_mul(
                        stat[:, 16:17], stat[:, 16:17], -1.0))
                    k.wait("scalar", t_mb)
                    k.wait("scalar", prev_tp[b2])
                    t_ap = k.op("scalar", lambda e, b2=b2: e.activation(
                        yt[:, b2, :], xt[:, b2, :], AF.Identity,
                        bias=stat[:, 16:17], scale=stat[:, 3:4]))
                    prev_apply[b2] = t_ap
                    k.wait("sync", t_ap)
                    if xi == 0:
                        k.wait("sync", last_qkv_mm)
                    t_tp = k.tdma("sync", lambda e, b2=b2, xi=xi: e.dma_start(
                        out=y_fm[:, :, xi * 128:(xi + 1) * 128],
                        in_=yt[:, b2, :], transpose=True))
                    prev_tp[b2] = t_tp
                    tp_ticks.append(t_tp)
                    pump(2)
                if tp_yfm is not None:
                    k.wait_all("gpsimd", tp_ticks)
                    k.dma("gpsimd", lambda e, seg=seg: e.dma_start(
                        out=tp_yfm[:, :, seg * 512:(seg + 1) * 512],
                        in_=y_fm[:]))

                k.wait_all("tensor", tp_ticks)
                if seg == 0:
                    k.wait_all("tensor", [t_wq, t_wk, t_wv])
                toff = seg * 512
                for oc in range(8):
                    wt = wq16 if oc < 4 else wk16
                    hh = oc % 4
                    dstt = q_fm if oc < 4 else k_fm
                    scl = INV_SQRT_HD if oc < 4 else 1.0
                    pb = pp_mm % 4
                    pp_mm += 1
                    k.wait("tensor", prev_evac[pb])
                    fns = []
                    for dc in range(DC):
                        fns.append(
                            lambda e, pb=pb, dc=dc, wt=wt, hh=hh:
                            e.matmul(ps_a[:, pb, :],
                                     wt[:, dc, hh * 128:(hh + 1) * 128],
                                     y_fm[:, dc, :],
                                     start=(dc == 0), stop=(dc == DC - 1)))
                    t_mm = k.chain("tensor", fns)
                    last_qkv_mm = t_mm
                    k.wait("scalar", t_mm)
                    t_ev = k.op("scalar", lambda e, pb=pb, dstt=dstt, hh=hh,
                                toff=toff, scl=scl: e.activation(
                                    dstt[:, hh, toff:toff + 512],
                                    ps_a[:, pb, :], AF.Copy, scale=scl))
                    prev_evac[pb] = t_ev
                    pump(1)
                for tsl in range(4):
                    pb = pp_mm % 4
                    pp_mm += 1
                    k.wait("tensor", prev_evac[pb])
                    fns = []
                    for dc in range(DC):
                        fns.append(lambda e, pb=pb, dc=dc, tsl=tsl: e.matmul(
                            ps_a[:, pb, :],
                            y_fm[:, dc, tsl * 128:(tsl + 1) * 128],
                            wv16[:, dc, :],
                            start=(dc == 0), stop=(dc == DC - 1)))
                    t_mm = k.chain("tensor", fns)
                    last_qkv_mm = t_mm
                    k.wait("scalar", t_mm)
                    tg = seg * 4 + tsl
                    t_ev = k.op("scalar", lambda e, pb=pb, tg=tg: e.activation(
                        v_tm[:, tg, :], ps_a[:, pb, :], AF.Copy, scale=1.0))
                    prev_evac[pb] = t_ev
                    pump(1)
            qkv_done = [t for t in prev_evac if t is not None]
            if tp_q is not None:
                k.wait_all("gpsimd", qkv_done)
                k.dma("gpsimd", lambda e: e.dma_start(out=tp_q, in_=q_fm[:]))
                k.dma("gpsimd", lambda e: e.dma_start(out=tp_k, in_=k_fm[:]))
                k.dma("gpsimd", lambda e: e.dma_start(out=tp_v, in_=v_tm[:]))
            k.flush()

        sB = ExitStack()
        with sB:
            attn_st = sB.enter_context(nc.sbuf_tensor("attn_st", [128, 2, 512], F16))
            p_buf = sB.enter_context(nc.sbuf_tensor("p_buf", [128, 2, S], F16))
            pt_buf = sB.enter_context(nc.sbuf_tensor(
                "pt_buf", [128, 2, 32, 4, 128], F16, is_transpose_dest=True))
            ps_sa = sB.enter_context(nc.psum_tensor("ps_sa", [128, 2048], F32))
            ps_sb = sB.enter_context(nc.psum_tensor("ps_sb", [128, 1024], F32))
            ps_at = sB.enter_context(nc.psum_tensor("ps_at", [128, 2, 512], F32))

            k.barrier()
            wave_owner = {"A": None, "B": None, "A2": None}
            prev_at_evac = [None, None]
            prev_ast_dma = [None, None]
            attn_dma_ticks = []
            prev_p_tp = [None, None]

            k.wait_all("tensor", qkv_done)
            k.wait("vector", t_mask)
            groups = [(h, g) for h in range(NHC) for g in range(8)]
            pend_v = None
            prev_vmm_buf = [None, None]
            wave_bufs = [ps_sa[:, 0:1024], ps_sa[:, 1024:2048], ps_sb[:, :]]
            wave_owner3 = [None, None, None]
            wq = 0

            def emit_v(hv, gv, tpt):
                pb = gv % 2
                gp = (hv * 8 + gv) % 2
                k.wait_all("tensor", tpt)
                k.wait("tensor", prev_at_evac[pb])
                fns = []
                nkt = 4 * gv + 4
                for kt in range(nkt):
                    j0 = max(0, kt - 4 * gv)
                    fns.append(lambda e, hv=hv, kt=kt, j0=j0, pb=pb, gp=gp,
                               nkt=nkt: e.matmul(
                                   ps_at[:, pb, j0 * 128:512],
                                   v_tm[:, kt, hv * 128:(hv + 1) * 128],
                                   pt_buf[:, gp, kt, j0:4, :],
                                   start=(kt == 0), stop=(kt == nkt - 1)))
                t_v = k.chain("tensor", fns)
                prev_vmm_buf[gp] = t_v
                k.wait("scalar", t_v)
                k.wait("scalar", prev_ast_dma[pb])
                t_ae = k.op("scalar", lambda e, pb=pb: e.activation(
                    attn_st[:, pb, :], ps_at[:, pb, :], AF.Copy, scale=1.0))
                prev_at_evac[pb] = t_ae
                k.wait("scalar", t_ae)
                t_ad = k.dma("scalar", lambda e, hv=hv, gv=gv, pb=pb:
                             e.dma_start(
                                 out=attn_d[hv * 128:(hv + 1) * 128,
                                            gv * 512:(gv + 1) * 512],
                                 in_=attn_st[:, pb, :]))
                prev_ast_dma[pb] = t_ad
                attn_dma_ticks.append(t_ad)

            for gi, (h, g) in enumerate(groups):
                gp = gi % 2
                tp_ticks = []
                for jj in range(4):
                    qi = 4 * g + jj
                    ti_g = gi * 4 + jj
                    tpar = ti_g % 2
                    base = 24 * tpar
                    q0 = qi * 128
                    kw = q0 + 128
                    nw = (kw + 1023) // 1024
                    exp_ticks = []
                    wave_info = []
                    for wv in range(nw):
                        ko = wv * 1024
                        wl = min(1024, kw - ko)
                        bid = (wq + wv) % 3
                        buf = wave_bufs[bid]
                        wave_info.append((ko, wl, bid))
                        k.wait("tensor", wave_owner3[bid])
                        fns = []
                        nsub = (wl + 511) // 512
                        for sb_i in range(nsub):
                            so = sb_i * 512
                            nn = min(512, wl - so)
                            fns.append(
                                lambda e, h=h, q0=q0, buf=buf, so=so,
                                nn=nn, ko=ko: e.matmul(
                                    buf[:, so:so + nn],
                                    q_fm[:, h, q0:q0 + 128],
                                    k_fm[:, h, ko + so:ko + so + nn],
                                    start=True, stop=True))
                        t_sc = k.chain("tensor", fns)
                        k.wait("vector", t_sc)
                        if wv == nw - 1:
                            dof = wl - 128
                            t_msk = k.op(
                                "vector", lambda e, buf=buf, dof=dof:
                                e.tensor_tensor(buf[:, dof:dof + 128],
                                                buf[:, dof:dof + 128],
                                                mask[:], ALU.add))
                            k.wait("vector", t_msk)
                        t_ng = k.op("vector", lambda e, buf=buf, wl=wl,
                                    c=base + wv: e.reduce_max(
                                        stat[:, c:c + 1],
                                        buf[:, 0:wl], axis=AX.X,
                                        negate=True))
                        k.wait("scalar", t_ng)
                        if wv == 0:
                            k.wait("scalar", prev_p_tp[tpar])
                        t_ex = k.op("scalar", lambda e, buf=buf, wl=wl,
                                    ko=ko, c=base + wv, a=base + 4 + wv,
                                    tpar=tpar: e.activation(
                                        p_buf[:, tpar, ko:ko + wl],
                                        buf[:, 0:wl],
                                        AF.Exp, bias=stat[:, c:c + 1],
                                        accum_out=stat[:, a:a + 1]))
                        wave_owner3[bid] = t_ex
                        exp_ticks.append(t_ex)
                    wq += nw
                    k.wait_all("vector", exp_ticks)
                    if nw == 1:
                        t_rho = k.op("vector", lambda e, base=base:
                                     e.reciprocal(stat[:, base + 19:base + 20],
                                                  stat[:, base + 4:base + 5]))
                    else:
                        t_mn = k.op("vector", lambda e, nw=nw, base=base:
                                    e.tensor_reduce(
                                        stat[:, base + 8:base + 9],
                                        stat[:, base:base + nw],
                                        axis=AX.X, op=ALU.min))
                        k.wait("scalar", t_mn)
                        t_al = k.op("scalar", lambda e, nw=nw, base=base:
                                    e.activation(
                                        stat[:, base + 9:base + 9 + nw],
                                        stat[:, base:base + nw], AF.Exp,
                                        scale=-1.0,
                                        bias=stat[:, base + 8:base + 9]))
                        k.wait("vector", t_al)
                        t_as = k.op("vector", lambda e, nw=nw, base=base:
                                    e.tensor_tensor(
                                        stat[:, base + 13:base + 13 + nw],
                                        stat[:, base + 9:base + 9 + nw],
                                        stat[:, base + 4:base + 4 + nw],
                                        ALU.mult))
                        k.wait("vector", t_as)
                        t_S = k.op("vector", lambda e, nw=nw, base=base:
                                   e.reduce_sum(
                                       stat[:, base + 17:base + 18],
                                       stat[:, base + 13:base + 13 + nw],
                                       axis=AX.X))
                        k.wait("vector", t_S)
                        t_R = k.op("vector", lambda e, base=base: e.reciprocal(
                            stat[:, base + 18:base + 19],
                            stat[:, base + 17:base + 18]))
                        k.wait("vector", t_R)
                        t_rho = k.op("vector", lambda e, nw=nw, base=base:
                                     e.tensor_scalar_mul(
                                         stat[:, base + 19:base + 19 + nw],
                                         stat[:, base + 9:base + 9 + nw],
                                         stat[:, base + 18:base + 19]))
                    k.wait("scalar", t_rho)
                    norm_ticks = []
                    for wv, (ko, wl, bid) in enumerate(wave_info):
                        t_nm = k.op("scalar", lambda e, ko=ko, wl=wl, tpar=tpar,
                                    rc=base + 19 + wv: e.activation(
                                        p_buf[:, tpar, ko:ko + wl],
                                        p_buf[:, tpar, ko:ko + wl], AF.Identity,
                                        scale=stat[:, rc:rc + 1]))
                        norm_ticks.append(t_nm)
                    k.wait_all("sync", norm_ticks)
                    if jj == 0:
                        k.wait("sync", prev_vmm_buf[gp])
                    ktn = kw // 128
                    t_tp = k.tdma("sync", lambda e, kw=kw, ktn=ktn, jj=jj,
                                  gp=gp, tpar=tpar: e.dma_start(
                                      out=pt_buf[:, gp, 0:ktn, jj, :],
                                      in_=p_buf[:, tpar, 0:kw], transpose=True))
                    tp_ticks.append(t_tp)
                    prev_p_tp[tpar] = t_tp
                if pend_v is not None:
                    emit_v(*pend_v)
                pend_v = (h, g, tp_ticks)
                pump(2)
            emit_v(*pend_v)
            if tp_attn is not None:
                k.wait_all("gpsimd", attn_dma_ticks)
                k.dma("gpsimd", lambda e: e.dma_start(out=tp_attn, in_=attn_d[:, :]))
            pump(10 ** 9)
            k.flush()

        sAB.__exit__(None, None, None)
        sQ.__exit__(None, None, None)

        x1 = st.enter_context(nc.sbuf_tensor("x1", [128, 8, D], F32))
        z_fm = st.enter_context(nc.sbuf_tensor("z_fm", [128, DC, TC], F16,
                                               is_transpose_dest=True))
        last_zfm = []
        sC = ExitStack()
        with sC:
            wo16 = sC.enter_context(nc.sbuf_tensor("wo16", [128, 4, D], F16))
            astr = sC.enter_context(nc.sbuf_tensor("astr", [128, 4, 2, 128], F16))
            prt16 = sC.enter_context(nc.sbuf_tensor("prt16", [128, 2, D], F16))
            o16 = sC.enter_context(nc.sbuf_tensor("o16", [128, 2, D], F16))
            xot = sC.enter_context(nc.sbuf_tensor("xot", [128, 2, D], F32))
            tmp32 = sC.enter_context(nc.sbuf_tensor("tmp32", [128, D], F32))
            zt = sC.enter_context(nc.sbuf_tensor("zt", [128, 2, D], F16))
            ps_o = sC.enter_context(nc.psum_tensor("ps_o", [128, 2, D], F32))

            k.barrier()
            t_wo = k.dma("gpsimd", lambda e: e.dma_start(
                out=wo16[:], in_=woTs.rearrange("(c p) n -> p c n", p=128)))
            k.wait_all("gpsimd", attn_dma_ticks)
            k.wait("tensor", t_wo)
            prev_pev = [None, None]
            prev_std = [None, None]
            prev_strip_mm = [None, None]
            rs_ticks = []
            ii2 = 0
            for o2 in range(8):
                for j4 in range(4):
                    tt = j4 * 8 + o2
                    pb = ii2 % 2
                    ii2 += 1
                    k.wait("sync", prev_strip_mm[pb])
                    if ii2 <= 2:
                        k.wait_all("sync", attn_dma_ticks)
                    t_as = k.dma("sync", lambda e, tt=tt, pb=pb: e.dma_start(
                        out=astr[:, :, pb, :],
                        in_=attn_d[:, tt * 128:(tt + 1) * 128].rearrange(
                            "(c p) t -> p c t", p=128)))
                    k.wait("tensor", prev_pev[pb])
                    k.wait("tensor", t_as)
                    fns = []
                    for sl in range(4):
                        for hc in range(4):
                            fns.append(lambda e, sl=sl, hc=hc, pb=pb:
                                       e.matmul(
                                           ps_o[:, pb, sl * 512:(sl + 1) * 512],
                                           astr[:, hc, pb, :],
                                           wo16[:, hc, sl * 512:(sl + 1) * 512],
                                           start=(hc == 0), stop=(hc == 3)))
                    t_mm = k.chain("tensor", fns)
                    prev_strip_mm[pb] = t_mm
                    k.wait("scalar", t_mm)
                    k.wait("scalar", prev_std[pb])
                    t_pe = k.op("scalar", lambda e, pb=pb: e.activation(
                        prt16[:, pb, :], ps_o[:, pb, :], AF.Copy, scale=1.0))
                    prev_pev[pb] = t_pe
                    k.wait("scalar", t_pe)
                    kch = o2 // 2
                    r0 = j4 * 256 + (o2 % 2) * 128
                    t_sd = k.dma("scalar", lambda e, r0=r0, pb=pb, kch=kch:
                                 e.dma_start(out=rs_in_c[kch][r0:r0 + 128, :],
                                             in_=prt16[:, pb, :]))
                    prev_std[pb] = t_sd
            k.wait("gpsimd", prev_std[0])
            k.wait("gpsimd", prev_std[1])
            for kch in range(4):
                if rs_ticks:
                    k.wait("gpsimd", rs_ticks[-1])
                t_rsk = k.cc(lambda e, kch=kch: e.collective_compute(
                    "ReduceScatter", ALU.add,
                    replica_groups=[[0, 1, 2, 3], [4, 5, 6, 7]],
                    ins=[rs_in_c[kch].opt()],
                    outs=[rs_out_c[kch].opt()]))
                rs_ticks.append(t_rsk)
            k.wait("gpsimd", rs_ticks[-1])
            t_bar_a = k.cc(lambda e: e.collective_compute(
                "AllReduce", ALU.add, replica_groups=[[0, 1, 2, 3, 4, 5, 6, 7]],
                ins=[asum_in.opt()], outs=[asum_out.opt()]))
            rs_gate = [t_bar_a, t_bar_a, t_bar_a, t_bar_a]

            prev_zt = [None, None]
            prev_use_o = [None, None]
            prev_use_x = [None, None]
            for tt in range(8):
                pb = tt % 2
                k.wait("sync", rs_gate[tt // 2])
                k.wait("sync", prev_use_o[pb])
                t_lo = k.dma("sync", lambda e, tt=tt, pb=pb: e.dma_start(
                    out=o16[:, pb, :],
                    in_=rs_out_c[tt // 2][(tt % 2) * 128:(tt % 2) * 128 + 128, :]))
                k.wait("scalar", prev_use_x[pb])
                t_lx = k.dma("scalar", lambda e, tt=tt, pb=pb: e.dma_start(
                    out=xot[:, pb, :], in_=x_own[tt * 128:(tt + 1) * 128, :]))
                k.wait("vector", t_lo)
                fns = []
                for q4 in range(4):
                    fns.append(lambda e, pb=pb, q4=q4: e.bn_stats(
                        bnst[:, q4 * 6:(q4 + 1) * 6],
                        o16[:, pb, q4 * 512:(q4 + 1) * 512]))
                t_bn = k.chain("vector", fns)
                k.wait("vector", t_bn)
                t_agg = k.op("vector", lambda e: e.bn_aggr(stat[:, 0:2], bnst[:]))
                k.wait("scalar", t_agg)
                t_rstd = k.chain("scalar", [
                    lambda e: e.activation(stat[:, 2:3], stat[:, 1:2],
                                           AF.Ln, bias=eps_col[:]),
                    lambda e: e.activation(stat[:, 3:4], stat[:, 2:3],
                                           AF.Exp, scale=-0.5),
                ])
                k.wait("vector", t_rstd)
                k.wait_all("vector", prev_use_o)
                t_mb0 = k.op("vector", lambda e: e.tensor_tensor(
                    stat[:, 16:17], stat[:, 0:1], stat[:, 3:4], ALU.mult))
                k.wait("vector", t_mb0)
                t_mb = k.op("vector", lambda e: e.tensor_scalar_mul(
                    stat[:, 16:17], stat[:, 16:17], -1.0))
                k.wait("scalar", t_mb)
                t_n = k.op("scalar", lambda e, pb=pb: e.activation(
                    tmp32[:], o16[:, pb, :], AF.Identity,
                    bias=stat[:, 16:17], scale=stat[:, 3:4]))
                prev_use_o[pb] = t_n
                k.wait("vector", t_lx)
                k.wait("vector", t_n)
                t_x1 = k.op("vector", lambda e, tt=tt, pb=pb: e.tensor_tensor(
                    x1[:, tt, :], tmp32[:], xot[:, pb, :], ALU.add))
                prev_use_x[pb] = t_x1
                k.wait("vector", t_x1)
                fns = []
                for q4 in range(4):
                    fns.append(lambda e, tt=tt, q4=q4: e.bn_stats(
                        bnst[:, q4 * 6:(q4 + 1) * 6],
                        x1[:, tt, q4 * 512:(q4 + 1) * 512]))
                t_bn2 = k.chain("vector", fns)
                k.wait("vector", t_bn2)
                t_agg2 = k.op("vector", lambda e: e.bn_aggr(stat[:, 4:6], bnst[:]))
                k.wait("scalar", t_agg2)
                t_rstd2 = k.chain("scalar", [
                    lambda e: e.activation(stat[:, 6:7], stat[:, 5:6],
                                           AF.Ln, bias=eps_col[:]),
                    lambda e: e.activation(stat[:, 7:8], stat[:, 6:7],
                                           AF.Exp, scale=-0.5),
                ])
                k.wait("vector", t_rstd2)
                k.wait_all("vector", [prev_zt[0], prev_zt[1]])
                t_mb2a = k.op("vector", lambda e: e.tensor_tensor(
                    stat[:, 17:18], stat[:, 4:5], stat[:, 7:8], ALU.mult))
                k.wait("vector", t_mb2a)
                t_mb2 = k.op("vector", lambda e: e.tensor_scalar_mul(
                    stat[:, 17:18], stat[:, 17:18], -1.0))
                k.wait("scalar", t_mb2)
                k.wait("scalar", prev_zt[pb])
                t_z = k.op("scalar", lambda e, tt=tt, pb=pb: e.activation(
                    zt[:, pb, :], x1[:, tt, :], AF.Identity,
                    bias=stat[:, 17:18], scale=stat[:, 7:8]))
                k.wait("sync", t_z)
                t_ztp = k.tdma("sync", lambda e, tt=tt, pb=pb: e.dma_start(
                    out=z_fm[:, :, tt * 128:(tt + 1) * 128],
                    in_=zt[:, pb, :], transpose=True))
                prev_zt[pb] = t_ztp
                last_zfm.append(t_ztp)
            if tp_x1 is not None:
                k.wait_all("sync", prev_use_x)
                k.dma("sync", lambda e: e.dma_start(out=tp_x1, in_=x1[:]))
            if tp_z is not None:
                k.wait_all("gpsimd", last_zfm)
                k.dma("gpsimd", lambda e: e.dma_start(out=tp_z, in_=z_fm[:]))
            k.flush()

        sD = ExitStack()
        with sD:
            h_fm = sD.enter_context(nc.sbuf_tensor("h_fm", [128, FC, 512], F16))
            t1s = sD.enter_context(nc.sbuf_tensor("t1s", [128, 4, 512], F16))
            t2s = sD.enter_context(nc.sbuf_tensor("t2s", [128, 2, 2, D], F16))
            Mbuf = sD.enter_context(nc.sbuf_tensor("Mbuf", [128, 512], F32))
            Bbc = sD.enter_context(nc.sbuf_tensor("Bbc", [128, 512], F16))
            bpb = sD.enter_context(nc.sbuf_tensor("bpb", [128, 8], F32))
            vscr = sD.enter_context(nc.sbuf_tensor("vscr", [128, 4, 512], F16))
            of32 = sD.enter_context(nc.sbuf_tensor("of32", [128, 2, D], F32))
            ps_f = sD.enter_context(nc.psum_tensor("ps_f", [128, 8, 512], F32))

            k.barrier()
            t_out_ticks = []
            prev_ep_act = None
            k.wait("tensor", t_id)
            prev_f2_mm = None
            prev_ep = {}
            prev_out = [None, None]
            for hf in range(2):
                toff = hf * 512
                t_mset = k.op("vector", lambda e: e.memset(Mbuf[:], 0.0))
                k.wait("vector", t_mset)
                prev_t1ld = [None, None, None, None]
                prev_maxu = None
                blk_mm = {}
                h_ev = {}
                li = 0
                for hblk in range(16):
                    sbase = (hblk % 2) * 4
                    for dc in range(DC):
                        pp = li % 4
                        li += 1
                        if hblk == 0 and dc < 4:
                            k.wait("scalar", qstate["t1_done"])
                        k.wait("scalar", blk_mm.get((hblk, dc - 4)))
                        if dc < 4:
                            k.wait("scalar", blk_mm.get((hblk - 1, DC - 4 + dc)))
                        t_ld = k.dma("scalar", lambda e, hblk=hblk, dc=dc, pp=pp:
                                     e.dma_start(
                                         out=t1s[:, pp, :],
                                         in_=t1c[dc * 128:(dc + 1) * 128,
                                                 hblk * 512:(hblk + 1) * 512]))
                        prev_t1ld[pp] = t_ld
                        k.wait("tensor", t_ld)
                        if hblk == 0 and dc == 0:
                            k.wait_all("tensor", last_zfm)
                            if hf == 1:
                                k.wait("tensor", prev_f2_mm)
                        if dc == 0:
                            for s4 in range(4):
                                k.wait("tensor", h_ev.get((hblk - 2, s4)))
                        fns = []
                        for s4 in range(4):
                            fns.append(
                                lambda e, dc=dc, pp=pp, s4=s4, toff=toff,
                                sbase=sbase: e.matmul(
                                    ps_f[:, sbase + s4, :],
                                    t1s[:, pp, s4 * 128:(s4 + 1) * 128],
                                    z_fm[:, dc, toff:toff + 512],
                                    start=(dc == 0), stop=(dc == DC - 1)))
                        t_mm = k.chain("tensor", fns)
                        blk_mm[(hblk, dc)] = t_mm
                    for s4 in range(4):
                        hc = hblk * 4 + s4
                        k.wait("scalar", blk_mm[(hblk, DC - 1)])
                        t_ev = k.op("scalar", lambda e, hc=hc, s4=s4,
                                    sbase=sbase: e.activation(
                                        h_fm[:, hc, :], ps_f[:, sbase + s4, :],
                                        AF.Relu, scale=rsbc[:, 0:1]))
                        h_ev[(hblk, s4)] = t_ev
                        k.wait("vector", t_ev)
                        k.wait("vector", prev_maxu)
                        t_mx = k.op("vector", lambda e, hc=hc: e.tensor_tensor(
                            Mbuf[:], Mbuf[:], h_fm[:, hc, :], ALU.max))
                        prev_maxu = t_mx
                st_ticks = []
                last_mc = None
                for tl in range(4):
                    tg = hf * 4 + tl
                    k.wait("tensor", prev_maxu)
                    k.wait("tensor", last_mc)
                    t_tp = k.op("tensor", lambda e, tl=tl: e.transpose(
                        ps_f[:, 0, 0:128], Mbuf[:, tl * 128:(tl + 1) * 128],
                        id32[:]))
                    k.wait("vector", t_tp)
                    t_mc = k.op("vector", lambda e: e.reduce_max(
                        stat[:, 0:1], ps_f[:, 0, 0:128], axis=AX.X))
                    last_mc = t_mc
                    k.wait("vector", t_mc)
                    t_x2 = k.op("vector", lambda e: e.tensor_tensor(
                        stat[:, 1:2], stat[:, 0:1], stat[:, 0:1], ALU.mult))
                    k.wait("vector", t_x2)
                    t_xm = k.op("vector", lambda e: e.tensor_scalar_max(
                        stat[:, 1:2], stat[:, 1:2], EPS))
                    k.wait("vector", t_xm)
                    t_bi = k.op("vector", lambda e: e.reciprocal(
                        stat[:, 2:3], stat[:, 1:2]))
                    k.wait("vector", t_bi)
                    t_B = k.op("vector", lambda e, tl=tl: e.tensor_scalar_mul(
                        stat[:, 8 + tl:9 + tl], stat[:, 2:3], 255.0))
                    t_bp = k.op("vector", lambda e, tg=tg: e.tensor_scalar(
                        bpb[:, tg:tg + 1], stat[:, 1:2], rsbc[:, 1:2],
                        1.0 / 255.0, ALU.mult, op1=ALU.mult))
                    st_ticks.append(t_bp)
                    k.wait("tensor", t_B)
                    t_tb = k.op("tensor", lambda e, tl=tl: e.transpose(
                        ps_f[0:1, 1, tl * 128:(tl + 1) * 128],
                        stat[:, 8 + tl:9 + tl], id32[:]))
                    st_ticks.append(t_tb)
                k.wait_all("scalar", st_ticks)
                t_br = k.op("scalar", lambda e: e.activation(
                    of32[0:1, 0, 0:512], ps_f[0:1, 1, :], AF.Copy, scale=1.0))
                k.wait("tensor", t_br)
                t_bbc = k.op("tensor", lambda e: e.matmul(
                    ps_f[:, 2, :], ones_row[:], of32[0:1, 0, 0:512],
                    start=True, stop=True))
                k.wait("vector", t_bbc)
                t_bbce = k.op("vector", lambda e: e.tensor_copy(
                    Bbc[:], ps_f[:, 2, :]))
                prev_q = None
                for g4 in range(FC // 4):
                    if g4 == 0:
                        k.wait("vector", t_bbce)
                    qa = []
                    for c4 in range(4):
                        hc = g4 * 4 + c4
                        t_qa = k.op("vector", lambda e, hc=hc, c4=c4:
                                    e.tensor_tensor(vscr[:, c4, :], Bbc[:],
                                                    h_fm[:, hc, :], ALU.mult))
                        qa.append(t_qa)
                    t_qm = None
                    for c4 in range(4):
                        hc = g4 * 4 + c4
                        k.wait("vector", qa[c4])
                        t_qm = k.op("vector", lambda e, hc=hc, c4=c4:
                                    e.tensor_tensor(h_fm[:, hc, :],
                                                    vscr[:, c4, :],
                                                    h_fm[:, hc, :], ALU.mult))
                    hv = h_fm[:, g4 * 4:(g4 + 1) * 4, :].rearrange(
                        "p a b -> p (a b)")
                    k.wait("vector", t_qm)
                    t_qc = k.op("vector", lambda e, hv=hv: e.tensor_scalar_add(
                        hv, hv, MAGIC))
                    k.wait("vector", t_qc)
                    t_q = k.op("vector", lambda e, hv=hv: e.tensor_scalar_sub(
                        hv, hv, MAGIC))
                    prev_q = t_q
                if tp_h is not None and hf == 0:
                    k.wait("gpsimd", prev_q)
                    k.dma("gpsimd", lambda e: e.dma_start(out=tp_h, in_=h_fm[:]))
                if tp_bp is not None and hf == 1:
                    k.wait("sync", prev_q)
                    k.dma("sync", lambda e: e.dma_start(out=tp_bp, in_=bpb[:]))

                for sub in range(2):
                    prev_t2ld = [None, None]
                    last_mm_chunk = {}
                    for cp2 in range(FC // 2):
                        pp = cp2 % 2
                        if cp2 < 2:
                            k.wait("scalar", qstate["t2_done"])
                        k.wait("scalar", last_mm_chunk.get(cp2 - 2))
                        t_l2 = k.dma("scalar", lambda e, cp2=cp2, pp=pp:
                                     e.dma_start(
                                         out=t2s[:, pp, :, :],
                                         in_=t2c[cp2 * 256:(cp2 + 1) * 256, :]
                                         .rearrange("(a p) n -> p a n", p=128)))
                        prev_t2ld[pp] = t_l2
                        k.wait("tensor", t_l2)
                        if cp2 == 0:
                            k.wait("tensor", prev_q)
                            k.wait("tensor", t_bbce)
                            k.wait_all("tensor", list(prev_ep.values()))
                        fns = []
                        for ci in range(2):
                            c = cp2 * 2 + ci
                            for tloc in range(2):
                                for sl in range(4):
                                    slot = tloc * 4 + sl
                                    ts_ = (sub * 2 + tloc) * 128
                                    fns.append(
                                        lambda e, c=c, ci=ci, pp=pp, slot=slot,
                                        ts_=ts_, sl=sl: e.matmul(
                                            ps_f[:, slot, :],
                                            h_fm[:, c, ts_:ts_ + 128],
                                            t2s[:, pp, ci,
                                                sl * 512:(sl + 1) * 512],
                                            start=(c == 0), stop=(c == FC - 1)))
                        t_mm = k.chain("tensor", fns)
                        last_mm_chunk[cp2] = t_mm
                        prev_f2_mm = t_mm
                    for tloc in range(2):
                        tg = hf * 4 + sub * 2 + tloc
                        pb = tloc
                        k.wait("scalar", prev_f2_mm)
                        k.wait("scalar", prev_out[pb])
                        t_a1 = k.op("scalar", lambda e, tloc=tloc, tg=tg, pb=pb:
                                    e.activation(
                                        of32[:, pb, :],
                                        ps_f[:, tloc * 4:(tloc + 1) * 4, :]
                                        .rearrange("p a b -> p (a b)"),
                                        AF.Identity, scale=bpb[:, tg:tg + 1]))
                        k.wait("vector", t_a1)
                        fns = []
                        for q4 in range(4):
                            fns.append(lambda e, pb=pb, q4=q4: e.bn_stats(
                                bnst[:, q4 * 6:(q4 + 1) * 6],
                                of32[:, pb, q4 * 512:(q4 + 1) * 512]))
                        t_bn = k.chain("vector", fns)
                        k.wait("vector", t_bn)
                        t_agg = k.op("vector", lambda e: e.bn_aggr(stat[:, 0:2], bnst[:]))
                        k.wait("scalar", t_agg)
                        t_rstd = k.chain("scalar", [
                            lambda e: e.activation(stat[:, 2:3], stat[:, 1:2],
                                                   AF.Ln, bias=eps_col[:]),
                            lambda e: e.activation(stat[:, 3:4], stat[:, 2:3],
                                                   AF.Exp, scale=-0.5),
                        ])
                        k.wait("vector", t_rstd)
                        k.wait("vector", prev_ep_act)
                        t_mb3a = k.op("vector", lambda e: e.tensor_tensor(
                            stat[:, 16:17], stat[:, 0:1], stat[:, 3:4],
                            ALU.mult))
                        k.wait("vector", t_mb3a)
                        t_mb3 = k.op("vector", lambda e: e.tensor_scalar_mul(
                            stat[:, 16:17], stat[:, 16:17], -1.0))
                        k.wait("scalar", t_mb3)
                        t_n = k.op("scalar", lambda e, pb=pb: e.activation(
                            of32[:, pb, :], of32[:, pb, :], AF.Identity,
                            bias=stat[:, 16:17], scale=stat[:, 3:4]))
                        prev_ep_act = t_n
                        k.wait("vector", t_n)
                        t_fin = k.op("vector", lambda e, tg=tg, pb=pb:
                                     e.tensor_tensor(of32[:, pb, :],
                                                     of32[:, pb, :],
                                                     x1[:, tg, :], ALU.add))
                        for sl4 in range(4):
                            prev_ep[tloc * 4 + sl4] = t_fin
                        k.wait("sync", t_fin)
                        t_o = k.dma("sync", lambda e, tg=tg, pb=pb: e.dma_start(
                            out=out[tg * 128:(tg + 1) * 128, :],
                            in_=of32[:, pb, :]))
                        t_out_ticks.append(t_o)
                        prev_out[pb] = t_o
            if tp_rsi is not None:
                k.barrier()
                for kk2 in range(4):
                    k.dma("gpsimd", lambda e, kk2=kk2: e.dma_start(
                        out=tp_rsi[kk2], in_=rs_in_c[kk2][:, :]))
                    k.dma("gpsimd", lambda e, kk2=kk2: e.dma_start(
                        out=tp_rso[kk2], in_=rs_out_c[kk2][:, :]))
            if tp_t1 is not None:
                k.wait("gpsimd", qstate["t1_done"])
                k.dma("gpsimd", lambda e: e.dma_start(out=tp_t1, in_=t1c[:, :]))
            if tp_t2 is not None:
                k.wait("gpsimd", qstate["t2_done"])
                k.dma("gpsimd", lambda e: e.dma_start(out=tp_t2, in_=t2c[:, :]))
            k.wait_all("sync", t_out_ticks)
            if any(t is not None for t in
                   (tp_e_z, tp_e_x1, tp_e_h, tp_e_bp, tp_e_bbc)):
                k.barrier()
                for tp, src_ap in ((tp_e_z, z_fm), (tp_e_x1, x1),
                                   (tp_e_h, h_fm), (tp_e_bp, bpb),
                                   (tp_e_bbc, Bbc)):
                    if tp is not None:
                        k.dma("gpsimd", lambda e, tp=tp, s=src_ap:
                              e.dma_start(out=tp, in_=s[:]))
                k.wait("sync", (("d", "gpsimd"), k.cnt[("d", "gpsimd")]))
            k.flush()

    return nc


def kernel(L0_attn_hidden_states, L0_attn_w_qkv, L0_attn_w_o,
           L0_ffn_c_fc, L0_ffn_c_proj, _taps=()):
    x = np.ascontiguousarray(np.asarray(L0_attn_hidden_states, dtype=np.float32))
    wqkv = np.asarray(L0_attn_w_qkv, dtype=np.float32)
    wo = np.asarray(L0_attn_w_o, dtype=np.float32)
    cfcv = np.ascontiguousarray(np.asarray(L0_ffn_c_fc, dtype=np.float32))
    cp = np.asarray(L0_ffn_c_proj, dtype=np.float32)

    cpTv = np.ascontiguousarray(cp.T)
    woT = np.ascontiguousarray(wo.T)
    ident = np.eye(128, dtype=np.float32)
    ii = np.arange(128)
    mask = np.where(ii[None, :] <= ii[:, None], 0.0, NEG).astype(np.float32)

    in_maps = []
    for c in range(8):
        b, j = divmod(c, 4)
        in_maps.append(dict(
            x_full=np.ascontiguousarray(x[b]),
            x_own=np.ascontiguousarray(x[b, TC * j:TC * (j + 1)]),
            wqT=np.ascontiguousarray(wqkv[512 * j:512 * (j + 1)].T),
            wkT=np.ascontiguousarray(wqkv[D + 512 * j:D + 512 * (j + 1)].T),
            wvT=np.ascontiguousarray(
                wqkv[2 * D + 512 * j:2 * D + 512 * (j + 1)].T),
            woTs=np.ascontiguousarray(woT[512 * j:512 * (j + 1)]),
            cfc=cfcv,
            cpT=cpTv,
            cfc_sl=np.ascontiguousarray(cfcv[256 * c:256 * (c + 1)]),
            cpT_sl=np.ascontiguousarray(cpTv[1024 * c:1024 * (c + 1)]),
            identD=ident,
            maskD=mask,
        ))

    nc = build_program(taps=_taps)
    res = run_bass_kernel_spmd(nc, in_maps, core_ids=list(range(8)))

    outv = np.empty((B, S, D), dtype=np.float32)
    for c in range(8):
        b, j = divmod(c, 4)
        outv[b, TC * j:TC * (j + 1)] = res.results[c]["out"]
    if _taps:
        kernel._tap_results = [
            {kk2: vv for kk2, vv in r.items() if kk2.startswith("tap_")}
            for r in res.results]
    return outv


# version 14
# speedup vs baseline: 1.0777x; 1.0777x over previous
import numpy as np
from contextlib import ExitStack

import concourse.bass as bass
import concourse.mybir as mybir
from concourse.bass_utils import run_bass_kernel_spmd

F32 = mybir.dt.float32
F16 = mybir.dt.float16
AF = mybir.ActivationFunctionType
ALU = mybir.AluOpType
AX = mybir.AxisListType

B, S, D, NH, HD, FF = 2, 4096, 2048, 16, 128, 8192
TC = 1024
NHC = 4
DC = D // 128
FC = FF // 128
NEG = -1.0e30
MAGIC = 1536.0
EPS = 1e-5
INV_SQRT_HD = float(1.0 / np.sqrt(128.0))
W_CNT = float(D * FF)

ENGINES = ("tensor", "vector", "scalar", "gpsimd", "sync")


class K:

    def __init__(self, nc, st):
        self.nc = nc
        self.blk = None
        self.buf = {e: [] for e in ENGINES}
        self.sem = {}
        self.cnt = {}
        for e in ENGINES:
            for kind in ("c", "d", "t"):
                self.sem[(kind, e)] = st.enter_context(nc.semaphore(f"{kind}_{e}"))
                self.cnt[(kind, e)] = 0
        self.sem["cc"] = st.enter_context(nc.semaphore("cc"))
        self.cnt["cc"] = 0

    def op(self, eng, fn, inc=True):
        key = ("c", eng)
        t = None
        if inc:
            self.cnt[key] += 1
            t = (key, self.cnt[key])
        self.buf[eng].append(("op", fn, key if inc else None, 1))
        return t

    def dma(self, eng, fn, kind="d"):
        key = (kind, eng)
        self.cnt[key] += 16
        t = (key, self.cnt[key])
        self.buf[eng].append(("op", fn, key, 16))
        return t

    def tdma(self, eng, fn):
        return self.dma(eng, fn, kind="t")

    def cc(self, fn):
        self.cnt["cc"] += 1
        t = ("cc", self.cnt["cc"])
        self.buf["gpsimd"].append(("op", fn, "cc", 1))
        return t

    def chain(self, eng, fns):
        for f in fns[:-1]:
            self.buf[eng].append(("op", f, None, 1))
        return self.op(eng, fns[-1])

    def wait(self, eng, t):
        if t is None:
            return
        self.buf[eng].append(("wait", t))

    def barrier(self):
        for e in ENGINES:
            for key, v in self.cnt.items():
                if v > 0 and key != ("c", e):
                    self.buf[e].append(("wait", (key, v)))

    def wait_all(self, eng, ts):
        best = {}
        for t in ts:
            if t is None:
                continue
            key, v = t
            if key not in best or v > best[key]:
                best[key] = v
        for key, v in best.items():
            self.buf[eng].append(("wait", (key, v)))

    def flush(self):
        for e in ENGINES:
            items = self.buf[e]
            if not items:
                continue
            self.buf[e] = []
            sems = self.sem

            def body(eng, items=items):
                for it in items:
                    if it[0] == "op":
                        _, fn, key, amt = it
                        instr = fn(eng)
                        if key is not None:
                            instr.then_inc(sems[key], amt)
                    else:
                        key, v = it[1]
                        eng.wait_ge(sems[key], v)

            getattr(self.blk, e)(body)


def build_program(taps=()):
    nc = bass.Bass("TRN2", target_bir_lowering=False)

    x_full = nc.dram_tensor("x_full", [S, D], F32, kind="ExternalInput").ap()
    x_own = nc.dram_tensor("x_own", [TC, D], F32, kind="ExternalInput").ap()
    wqT = nc.dram_tensor("wqT", [D, 512], F32, kind="ExternalInput").ap()
    wkT = nc.dram_tensor("wkT", [D, 512], F32, kind="ExternalInput").ap()
    wvT = nc.dram_tensor("wvT", [D, 512], F32, kind="ExternalInput").ap()
    woTs = nc.dram_tensor("woTs", [512, D], F32, kind="ExternalInput").ap()
    cfc = nc.dram_tensor("cfc", [D, FF], F32, kind="ExternalInput").ap()
    cpT = nc.dram_tensor("cpT", [FF, D], F32, kind="ExternalInput").ap()
    cfc_sl = nc.dram_tensor("cfc_sl", [256, FF], F32, kind="ExternalInput").ap()
    cpT_sl = nc.dram_tensor("cpT_sl", [1024, D], F32, kind="ExternalInput").ap()
    identD = nc.dram_tensor("identD", [128, 128], F32, kind="ExternalInput").ap()
    maskD = nc.dram_tensor("maskD", [128, 128], F32, kind="ExternalInput").ap()
    out = nc.dram_tensor("out", [TC, D], F32, kind="ExternalOutput").ap()

    asum_in = nc.dram_tensor("asum_in", [128, 2], F32, kind="Internal",
                             addr_space="Local").ap()
    asum_out = nc.dram_tensor("asum_out", [128, 2], F32, kind="Internal",
                              addr_space="Local").ap()
    rs_in_c = [nc.dram_tensor(f"rs_in{kk}", [TC, D], F16, kind="Internal",
                              addr_space="Local").ap() for kk in range(4)]
    rs_out_c = [nc.dram_tensor(f"rs_out{kk}", [256, D], F16, kind="Internal",
                               addr_space="Local").ap() for kk in range(4)]
    t1c = nc.dram_tensor("t1c", [D, FF], F16, kind="Internal",
                         addr_space="Local").ap()
    t2c = nc.dram_tensor("t2c", [FF, D], F16, kind="Internal",
                         addr_space="Local").ap()
    attn_d = nc.dram_tensor("attn_d", [NHC * HD, S], F16, kind="Internal",
                            addr_space="Local").ap()

    tap_ap = {}

    def mk_tap(name, shape, dtype):
        if name in taps:
            tap_ap[name] = nc.dram_tensor("tap_" + name, shape, dtype,
                                          kind="ExternalOutput").ap()
            return tap_ap[name]
        return None

    tp_yfm = mk_tap("yfm", [128, DC, S], F32)
    tp_q = mk_tap("q", [128, NHC, S], F32)
    tp_k = mk_tap("k", [128, NHC, S], F32)
    tp_v = mk_tap("v", [128, S // 128, 512], F32)
    tp_attn = mk_tap("attn", [NHC * HD, S], F32)
    tp_rs = mk_tap("rs", [128, 4], F32)
    tp_t1 = mk_tap("t1", [D, FF], F32)
    tp_t2 = mk_tap("t2", [FF, D], F32)
    tp_x1 = mk_tap("x1", [128, 8, D], F32)
    tp_z = mk_tap("z", [128, DC, TC], F32)
    tp_h = mk_tap("h", [128, FC, 512], F32)
    tp_bp = mk_tap("bp", [128, 8], F32)
    tp_e_z = mk_tap("e_z", [128, DC, TC], F32)
    tp_e_x1 = mk_tap("e_x1", [128, 8, D], F32)
    tp_e_h = mk_tap("e_h", [128, FC, 512], F32)
    tp_e_bp = mk_tap("e_bp", [128, 8], F32)
    tp_e_bbc = mk_tap("e_bbc", [128, 512], F32)
    tp_rsi = mk_tap("rsi", [4, TC, D], F32)
    tp_rso = mk_tap("rso", [4, 256, D], F32)

    with ExitStack() as st:
        id32 = st.enter_context(nc.sbuf_tensor("id32", [128, 128], F32))
        ones_row = st.enter_context(nc.sbuf_tensor("ones_row", [1, 128], F32))
        eps_col = st.enter_context(nc.sbuf_tensor("eps_col", [128, 1], F32))
        magic_col = st.enter_context(nc.sbuf_tensor("magic_col", [128, 1], F32))
        rsbc = st.enter_context(nc.sbuf_tensor("rsbc", [128, 4], F32))
        stat = st.enter_context(nc.sbuf_tensor("stat", [128, 64], F32))
        bnst = st.enter_context(nc.sbuf_tensor("bnst", [128, 24], F32))
        srow = st.enter_context(nc.sbuf_tensor("srow", [1, 4], F32))

        k = K(nc, st)
        blk = st.enter_context(nc.Block())
        k.blk = blk

        t_id = k.dma("sync", lambda e: e.dma_start(out=id32[:], in_=identD[:]))
        t_onesr = k.op("vector", lambda e: e.memset(ones_row[:], 1.0))
        t_eps = k.op("vector", lambda e: e.memset(eps_col[:], EPS))
        t_mgc = k.op("vector", lambda e: e.memset(magic_col[:], MAGIC))

        sS = ExitStack()
        sS.__enter__()
        ones_col = sS.enter_context(nc.sbuf_tensor("ones_col", [128, 1], F32))
        awb = sS.enter_context(nc.sbuf_tensor("awb", [128, 2, 2048], F32))
        adisc = sS.enter_context(nc.sbuf_tensor("adisc", [128, 2048], F16))
        acol = sS.enter_context(nc.sbuf_tensor("acol", [128, 16], F32))
        prt = sS.enter_context(nc.sbuf_tensor("prt", [128, 2], F32))
        ps_s = sS.enter_context(nc.psum_tensor("ps_s", [128, 8], F32))
        t_onesc = k.op("vector", lambda e: e.memset(ones_col[:], 1.0))

        tiles = []
        for i in range(2):
            for cb in range(4):
                tiles.append(cfc_sl[i * 128:(i + 1) * 128,
                                    cb * 2048:(cb + 1) * 2048])
        for i in range(8):
            tiles.append(cpT_sl[i * 128:(i + 1) * 128, :])
        prev_act = [None, None]
        for idx, src in enumerate(tiles):
            pp = idx % 2
            t_ld = k.dma("scalar", lambda e, src=src, pp=pp: e.dma_start(
                out=awb[:, pp, :], in_=src))
            k.wait("scalar", t_ld)
            t_a = k.op("scalar", lambda e, pp=pp, idx=idx: e.activation(
                adisc[:], awb[:, pp, :], AF.Abs,
                accum_out=acol[:, idx:idx + 1]))
            prev_act[pp] = t_a
        k.wait_all("vector", prev_act)
        t_p1 = k.op("vector", lambda e: e.reduce_sum(
            prt[:, 0:1], acol[:, 0:8], axis=AX.X))
        t_p2 = k.op("vector", lambda e: e.reduce_sum(
            prt[:, 1:2], acol[:, 8:16], axis=AX.X))
        k.wait_all("sync", [t_p1, t_p2])
        t_st = k.dma("sync", lambda e: e.dma_start(out=asum_in[:], in_=prt[:]))
        k.wait("gpsimd", t_st)
        t_cc0 = k.cc(lambda e: e.collective_compute(
            "AllReduce", ALU.add, replica_groups=[[0, 1, 2, 3, 4, 5, 6, 7]],
            ins=[asum_in.opt()], outs=[asum_out.opt()]))
        t_cc = k.cc(lambda e: e.collective_compute(
            "AllReduce", ALU.add, replica_groups=[[0, 1, 2, 3, 4, 5, 6, 7]],
            ins=[asum_in.opt()], outs=[asum_out.opt()]))
        k.wait("sync", t_cc)
        t_ld2 = k.dma("sync", lambda e: e.dma_start(out=prt[:], in_=asum_out[:]))
        k.wait("tensor", t_ld2)
        k.wait("tensor", t_onesc)
        t_tot = k.op("tensor", lambda e: e.matmul(
            ps_s[0:1, 0:2], ones_col[:], prt[:], start=True, stop=True))
        k.wait("vector", t_tot)
        t_s = k.op("vector", lambda e: e.tensor_scalar_mul(
            srow[0:1, 0:2], ps_s[0:1, 0:2], 1.0 / W_CNT))
        k.wait("vector", t_s)
        t_r = k.op("vector", lambda e: e.reciprocal(
            srow[0:1, 2:4], srow[0:1, 0:2]))
        k.wait_all("tensor", [t_s, t_r, t_onesr])
        t_bc = k.op("tensor", lambda e: e.matmul(
            ps_s[:, 4:8], ones_row[:], srow[:], start=True, stop=True))
        k.wait("vector", t_bc)
        t_rsbc = k.op("vector", lambda e: e.tensor_copy(rsbc[:], ps_s[:, 4:8]))
        if tp_rs is not None:
            k.wait("sync", t_rsbc)
            k.dma("sync", lambda e: e.dma_start(out=tp_rs, in_=rsbc[:]))
        k.flush()
        sS.__exit__(None, None, None)

        sQ = ExitStack()
        sQ.__enter__()
        qin = sQ.enter_context(nc.sbuf_tensor("qin", [128, 2, 512], F32))
        qmid = sQ.enter_context(nc.sbuf_tensor("qmid", [128, 2, 512], F32))
        qot = sQ.enter_context(nc.sbuf_tensor("qot", [128, 2, 512], F16))
        qstate = {"t1_done": None, "t2_done": None}

        def quant_gen():
            jobs = []
            for dc in range(DC):
                for cb in range(16):
                    jobs.append((cfc[dc * 128:(dc + 1) * 128,
                                     cb * 512:(cb + 1) * 512],
                                 t1c[dc * 128:(dc + 1) * 128,
                                     cb * 512:(cb + 1) * 512], 0))
            for hc in range(FC):
                for cb in range(4):
                    jobs.append((cpT[hc * 128:(hc + 1) * 128,
                                     cb * 512:(cb + 1) * 512],
                                 t2c[hc * 128:(hc + 1) * 128,
                                     cb * 512:(cb + 1) * 512], 1))
            prev_p1 = [None, None]
            prev_p1b = [None, None]
            prev_stq = [None, None]
            n_done = 0
            for ji, (src, dst, wi) in enumerate(jobs):
                pp = ji % 2
                k.wait("scalar", prev_p1[pp])
                t_ld = k.dma("scalar", lambda e, src=src, pp=pp: e.dma_start(
                    out=qin[:, pp, :], in_=src))
                k.wait("scalar", t_ld)
                k.wait("scalar", prev_p1b[pp])
                if ji == 0:
                    k.wait("scalar", t_rsbc)
                    k.wait("scalar", t_mgc)
                t_p1 = k.op("scalar", lambda e, pp=pp, wi=wi: e.activation(
                    qmid[:, pp, :], qin[:, pp, :], AF.Identity,
                    bias=magic_col[:], scale=rsbc[:, 2 + wi:3 + wi]))
                prev_p1[pp] = t_p1
                k.wait("vector", t_p1)
                k.wait("vector", prev_stq[pp])
                t_p2 = k.op("vector", lambda e, pp=pp: e.tensor_scalar(
                    qot[:, pp, :], qmid[:, pp, :], MAGIC - 1.0, MAGIC + 1.0,
                    ALU.max, op1=ALU.min))
                k.wait("vector", t_p2)
                t_p3 = k.op("vector", lambda e, pp=pp: e.tensor_scalar_sub(
                    qot[:, pp, :], qot[:, pp, :], MAGIC))
                prev_p1b[pp] = t_p3
                k.wait("gpsimd", t_p3)
                t_stq = k.dma("gpsimd", lambda e, dst=dst, pp=pp: e.dma_start(
                    out=dst, in_=qot[:, pp, :]))
                prev_stq[pp] = t_stq
                n_done += 1
                if n_done == 256:
                    qstate["t1_done"] = t_stq
                if n_done == 512:
                    qstate["t2_done"] = t_stq
                yield

        qg = quant_gen()

        def pump(n):
            for _ in range(n):
                if next(qg, "END") == "END":
                    break

        sAB = ExitStack()
        sAB.__enter__()
        mask = sAB.enter_context(nc.sbuf_tensor("mask", [128, 128], F32))
        q_fm = sAB.enter_context(nc.sbuf_tensor("q_fm", [128, NHC, S], F16))
        k_fm = sAB.enter_context(nc.sbuf_tensor("k_fm", [128, NHC, S], F16))
        v_tm = sAB.enter_context(nc.sbuf_tensor("v_tm", [128, S // 128, 512], F16))
        t_mask = k.dma("sync", lambda e: e.dma_start(out=mask[:], in_=maskD[:]))

        sA = ExitStack()
        with sA:
            xt = sA.enter_context(nc.sbuf_tensor("xt", [128, 2, D], F32))
            yt = sA.enter_context(nc.sbuf_tensor("yt", [128, 2, D], F16))
            y_fm = sA.enter_context(nc.sbuf_tensor(
                "y_fm", [128, DC, 512], F16, is_transpose_dest=True))
            wq16 = sA.enter_context(nc.sbuf_tensor("wq16", [128, DC, 512], F16))
            wk16 = sA.enter_context(nc.sbuf_tensor("wk16", [128, DC, 512], F16))
            wv16 = sA.enter_context(nc.sbuf_tensor("wv16", [128, DC, 512], F16))
            ps_a = sA.enter_context(nc.psum_tensor("ps_a", [128, 4, 512], F32))

            t_wq = k.dma("gpsimd", lambda e: e.dma_start(
                out=wq16[:], in_=wqT.rearrange("(c p) n -> p c n", p=128)))
            t_wk = k.dma("gpsimd", lambda e: e.dma_start(
                out=wk16[:], in_=wkT.rearrange("(c p) n -> p c n", p=128)))
            t_wv = k.dma("gpsimd", lambda e: e.dma_start(
                out=wv16[:], in_=wvT.rearrange("(c p) n -> p c n", p=128)))

            k.wait("scalar", t_eps)
            prev_apply = [None, None]
            prev_tp = [None, None]
            last_qkv_mm = None
            pp_mm = 0
            prev_evac = [None, None, None, None]
            qkv_done = []

            for seg in range(8):
                tp_ticks = []
                for xi in range(4):
                    ti = seg * 4 + xi
                    b2 = ti % 2
                    k.wait("sync", prev_apply[b2])
                    t_x = k.dma("sync", lambda e, ti=ti, b2=b2: e.dma_start(
                        out=xt[:, b2, :], in_=x_full[ti * 128:(ti + 1) * 128, :]))
                    k.wait("vector", t_x)
                    fns = []
                    for q4 in range(4):
                        fns.append(lambda e, b2=b2, q4=q4: e.bn_stats(
                            bnst[:, q4 * 6:(q4 + 1) * 6],
                            xt[:, b2, q4 * 512:(q4 + 1) * 512]))
                    t_bn = k.chain("vector", fns)
                    k.wait("vector", t_bn)
                    t_agg = k.op("vector", lambda e: e.bn_aggr(stat[:, 0:2], bnst[:]))
                    k.wait("scalar", t_agg)
                    t_rstd = k.chain("scalar", [
                        lambda e: e.activation(stat[:, 2:3], stat[:, 1:2],
                                               AF.Ln, bias=eps_col[:]),
                        lambda e: e.activation(stat[:, 3:4], stat[:, 2:3],
                                               AF.Exp, scale=-0.5),
                    ])
                    k.wait("vector", t_rstd)
                    k.wait_all("vector", prev_apply)
                    t_mb0 = k.op("vector", lambda e: e.tensor_tensor(
                        stat[:, 16:17], stat[:, 0:1], stat[:, 3:4], ALU.mult))
                    k.wait("vector", t_mb0)
                    t_mb = k.op("vector", lambda e: e.tensor_scalar_mul(
                        stat[:, 16:17], stat[:, 16:17], -1.0))
                    k.wait("scalar", t_mb)
                    k.wait("scalar", prev_tp[b2])
                    t_ap = k.op("scalar", lambda e, b2=b2: e.activation(
                        yt[:, b2, :], xt[:, b2, :], AF.Identity,
                        bias=stat[:, 16:17], scale=stat[:, 3:4]))
                    prev_apply[b2] = t_ap
                    k.wait("sync", t_ap)
                    if xi == 0:
                        k.wait("sync", last_qkv_mm)
                    t_tp = k.tdma("sync", lambda e, b2=b2, xi=xi: e.dma_start(
                        out=y_fm[:, :, xi * 128:(xi + 1) * 128],
                        in_=yt[:, b2, :], transpose=True))
                    prev_tp[b2] = t_tp
                    tp_ticks.append(t_tp)
                    pump(7)
                if tp_yfm is not None:
                    k.wait_all("gpsimd", tp_ticks)
                    k.dma("gpsimd", lambda e, seg=seg: e.dma_start(
                        out=tp_yfm[:, :, seg * 512:(seg + 1) * 512],
                        in_=y_fm[:]))

                k.wait_all("tensor", tp_ticks)
                if seg == 0:
                    k.wait_all("tensor", [t_wq, t_wk, t_wv])
                toff = seg * 512
                for oc in range(8):
                    wt = wq16 if oc < 4 else wk16
                    hh = oc % 4
                    dstt = q_fm if oc < 4 else k_fm
                    scl = INV_SQRT_HD if oc < 4 else 1.0
                    pb = pp_mm % 4
                    pp_mm += 1
                    k.wait("tensor", prev_evac[pb])
                    fns = []
                    for dc in range(DC):
                        fns.append(
                            lambda e, pb=pb, dc=dc, wt=wt, hh=hh:
                            e.matmul(ps_a[:, pb, :],
                                     wt[:, dc, hh * 128:(hh + 1) * 128],
                                     y_fm[:, dc, :],
                                     start=(dc == 0), stop=(dc == DC - 1)))
                    t_mm = k.chain("tensor", fns)
                    last_qkv_mm = t_mm
                    k.wait("scalar", t_mm)
                    t_ev = k.op("scalar", lambda e, pb=pb, dstt=dstt, hh=hh,
                                toff=toff, scl=scl: e.activation(
                                    dstt[:, hh, toff:toff + 512],
                                    ps_a[:, pb, :], AF.Copy, scale=scl))
                    prev_evac[pb] = t_ev
                    pump(1)
                for tsl in range(4):
                    pb = pp_mm % 4
                    pp_mm += 1
                    k.wait("tensor", prev_evac[pb])
                    fns = []
                    for dc in range(DC):
                        fns.append(lambda e, pb=pb, dc=dc, tsl=tsl: e.matmul(
                            ps_a[:, pb, :],
                            y_fm[:, dc, tsl * 128:(tsl + 1) * 128],
                            wv16[:, dc, :],
                            start=(dc == 0), stop=(dc == DC - 1)))
                    t_mm = k.chain("tensor", fns)
                    last_qkv_mm = t_mm
                    k.wait("scalar", t_mm)
                    tg = seg * 4 + tsl
                    t_ev = k.op("scalar", lambda e, pb=pb, tg=tg: e.activation(
                        v_tm[:, tg, :], ps_a[:, pb, :], AF.Copy, scale=1.0))
                    prev_evac[pb] = t_ev
                    pump(1)
            qkv_done = [t for t in prev_evac if t is not None]
            if tp_q is not None:
                k.wait_all("gpsimd", qkv_done)
                k.dma("gpsimd", lambda e: e.dma_start(out=tp_q, in_=q_fm[:]))
                k.dma("gpsimd", lambda e: e.dma_start(out=tp_k, in_=k_fm[:]))
                k.dma("gpsimd", lambda e: e.dma_start(out=tp_v, in_=v_tm[:]))
            k.flush()

        sB = ExitStack()
        with sB:
            attn_st = sB.enter_context(nc.sbuf_tensor("attn_st", [128, 2, 512], F16))
            p_buf = sB.enter_context(nc.sbuf_tensor("p_buf", [128, 2, S], F16))
            pt_buf = sB.enter_context(nc.sbuf_tensor(
                "pt_buf", [128, 2, 32, 4, 128], F16, is_transpose_dest=True))
            ps_sa = sB.enter_context(nc.psum_tensor("ps_sa", [128, 2048], F32))
            ps_sb = sB.enter_context(nc.psum_tensor("ps_sb", [128, 1024], F32))
            ps_at = sB.enter_context(nc.psum_tensor("ps_at", [128, 2, 512], F32))

            k.barrier()
            wave_owner = {"A": None, "B": None, "A2": None}
            prev_at_evac = [None, None]
            prev_ast_dma = [None, None]
            attn_dma_ticks = []
            prev_p_tp = [None, None]

            k.wait_all("tensor", qkv_done)
            k.wait("vector", t_mask)
            groups = [(h, g) for h in range(NHC) for g in range(8)]
            pend_v = None
            prev_vmm_buf = [None, None]
            wave_bufs = [ps_sa[:, 0:1024], ps_sa[:, 1024:2048], ps_sb[:, :]]
            wave_owner3 = [None, None, None]
            wq = 0

            def emit_v(hv, gv, tpt):
                pb = gv % 2
                gp = (hv * 8 + gv) % 2
                k.wait_all("tensor", tpt)
                k.wait("tensor", prev_at_evac[pb])
                fns = []
                nkt = 4 * gv + 4
                for kt in range(nkt):
                    j0 = max(0, kt - 4 * gv)
                    fns.append(lambda e, hv=hv, kt=kt, j0=j0, pb=pb, gp=gp,
                               nkt=nkt: e.matmul(
                                   ps_at[:, pb, j0 * 128:512],
                                   v_tm[:, kt, hv * 128:(hv + 1) * 128],
                                   pt_buf[:, gp, kt, j0:4, :],
                                   start=(kt == 0), stop=(kt == nkt - 1)))
                t_v = k.chain("tensor", fns)
                prev_vmm_buf[gp] = t_v
                k.wait("scalar", t_v)
                k.wait("scalar", prev_ast_dma[pb])
                t_ae = k.op("scalar", lambda e, pb=pb: e.activation(
                    attn_st[:, pb, :], ps_at[:, pb, :], AF.Copy, scale=1.0))
                prev_at_evac[pb] = t_ae
                k.wait("scalar", t_ae)
                t_ad = k.dma("scalar", lambda e, hv=hv, gv=gv, pb=pb:
                             e.dma_start(
                                 out=attn_d[hv * 128:(hv + 1) * 128,
                                            gv * 512:(gv + 1) * 512],
                                 in_=attn_st[:, pb, :]))
                prev_ast_dma[pb] = t_ad
                attn_dma_ticks.append(t_ad)

            for gi, (h, g) in enumerate(groups):
                gp = gi % 2
                tp_ticks = []
                for jj in range(4):
                    qi = 4 * g + jj
                    ti_g = gi * 4 + jj
                    tpar = ti_g % 2
                    base = 24 * tpar
                    q0 = qi * 128
                    kw = q0 + 128
                    nw = (kw + 1023) // 1024
                    exp_ticks = []
                    wave_info = []
                    for wv in range(nw):
                        ko = wv * 1024
                        wl = min(1024, kw - ko)
                        bid = (wq + wv) % 3
                        buf = wave_bufs[bid]
                        wave_info.append((ko, wl, bid))
                        k.wait("tensor", wave_owner3[bid])
                        fns = []
                        nsub = (wl + 511) // 512
                        for sb_i in range(nsub):
                            so = sb_i * 512
                            nn = min(512, wl - so)
                            fns.append(
                                lambda e, h=h, q0=q0, buf=buf, so=so,
                                nn=nn, ko=ko: e.matmul(
                                    buf[:, so:so + nn],
                                    q_fm[:, h, q0:q0 + 128],
                                    k_fm[:, h, ko + so:ko + so + nn],
                                    start=True, stop=True))
                        t_sc = k.chain("tensor", fns)
                        k.wait("vector", t_sc)
                        if wv == nw - 1:
                            dof = wl - 128
                            t_msk = k.op(
                                "vector", lambda e, buf=buf, dof=dof:
                                e.tensor_tensor(buf[:, dof:dof + 128],
                                                buf[:, dof:dof + 128],
                                                mask[:], ALU.add))
                            k.wait("vector", t_msk)
                        t_ng = k.op("vector", lambda e, buf=buf, wl=wl,
                                    c=base + wv: e.reduce_max(
                                        stat[:, c:c + 1],
                                        buf[:, 0:wl], axis=AX.X,
                                        negate=True))
                        k.wait("scalar", t_ng)
                        if wv == 0:
                            k.wait("scalar", prev_p_tp[tpar])
                        t_ex = k.op("scalar", lambda e, buf=buf, wl=wl,
                                    ko=ko, c=base + wv, a=base + 4 + wv,
                                    tpar=tpar: e.activation(
                                        p_buf[:, tpar, ko:ko + wl],
                                        buf[:, 0:wl],
                                        AF.Exp, bias=stat[:, c:c + 1],
                                        accum_out=stat[:, a:a + 1]))
                        wave_owner3[bid] = t_ex
                        exp_ticks.append(t_ex)
                    wq += nw
                    k.wait_all("vector", exp_ticks)
                    if nw == 1:
                        t_rho = k.op("vector", lambda e, base=base:
                                     e.reciprocal(stat[:, base + 19:base + 20],
                                                  stat[:, base + 4:base + 5]))
                    else:
                        t_mn = k.op("vector", lambda e, nw=nw, base=base:
                                    e.tensor_reduce(
                                        stat[:, base + 8:base + 9],
                                        stat[:, base:base + nw],
                                        axis=AX.X, op=ALU.min))
                        k.wait("scalar", t_mn)
                        t_al = k.op("scalar", lambda e, nw=nw, base=base:
                                    e.activation(
                                        stat[:, base + 9:base + 9 + nw],
                                        stat[:, base:base + nw], AF.Exp,
                                        scale=-1.0,
                                        bias=stat[:, base + 8:base + 9]))
                        k.wait("vector", t_al)
                        t_as = k.op("vector", lambda e, nw=nw, base=base:
                                    e.tensor_tensor(
                                        stat[:, base + 13:base + 13 + nw],
                                        stat[:, base + 9:base + 9 + nw],
                                        stat[:, base + 4:base + 4 + nw],
                                        ALU.mult))
                        k.wait("vector", t_as)
                        t_S = k.op("vector", lambda e, nw=nw, base=base:
                                   e.reduce_sum(
                                       stat[:, base + 17:base + 18],
                                       stat[:, base + 13:base + 13 + nw],
                                       axis=AX.X))
                        k.wait("vector", t_S)
                        t_R = k.op("vector", lambda e, base=base: e.reciprocal(
                            stat[:, base + 18:base + 19],
                            stat[:, base + 17:base + 18]))
                        k.wait("vector", t_R)
                        t_rho = k.op("vector", lambda e, nw=nw, base=base:
                                     e.tensor_scalar_mul(
                                         stat[:, base + 19:base + 19 + nw],
                                         stat[:, base + 9:base + 9 + nw],
                                         stat[:, base + 18:base + 19]))
                    k.wait("scalar", t_rho)
                    norm_ticks = []
                    for wv, (ko, wl, bid) in enumerate(wave_info):
                        t_nm = k.op("scalar", lambda e, ko=ko, wl=wl, tpar=tpar,
                                    rc=base + 19 + wv: e.activation(
                                        p_buf[:, tpar, ko:ko + wl],
                                        p_buf[:, tpar, ko:ko + wl], AF.Identity,
                                        scale=stat[:, rc:rc + 1]))
                        norm_ticks.append(t_nm)
                    k.wait_all("sync", norm_ticks)
                    if jj == 0:
                        k.wait("sync", prev_vmm_buf[gp])
                    ktn = kw // 128
                    t_tp = k.tdma("sync", lambda e, kw=kw, ktn=ktn, jj=jj,
                                  gp=gp, tpar=tpar: e.dma_start(
                                      out=pt_buf[:, gp, 0:ktn, jj, :],
                                      in_=p_buf[:, tpar, 0:kw], transpose=True))
                    tp_ticks.append(t_tp)
                    prev_p_tp[tpar] = t_tp
                if pend_v is not None:
                    emit_v(*pend_v)
                pend_v = (h, g, tp_ticks)
                pump(4)
            emit_v(*pend_v)
            if tp_attn is not None:
                k.wait_all("gpsimd", attn_dma_ticks)
                k.dma("gpsimd", lambda e: e.dma_start(out=tp_attn, in_=attn_d[:, :]))
            pump(10 ** 9)
            k.flush()

        sAB.__exit__(None, None, None)
        sQ.__exit__(None, None, None)

        x1 = st.enter_context(nc.sbuf_tensor("x1", [128, 8, D], F32))
        z_fm = st.enter_context(nc.sbuf_tensor("z_fm", [128, DC, TC], F16,
                                               is_transpose_dest=True))
        last_zfm = []
        sC = ExitStack()
        with sC:
            wo16 = sC.enter_context(nc.sbuf_tensor("wo16", [128, 4, D], F16))
            astr = sC.enter_context(nc.sbuf_tensor("astr", [128, 4, 2, 128], F16))
            prt16 = sC.enter_context(nc.sbuf_tensor("prt16", [128, 2, D], F16))
            o16 = sC.enter_context(nc.sbuf_tensor("o16", [128, 2, D], F16))
            xot = sC.enter_context(nc.sbuf_tensor("xot", [128, 2, D], F32))
            tmp32 = sC.enter_context(nc.sbuf_tensor("tmp32", [128, D], F32))
            zt = sC.enter_context(nc.sbuf_tensor("zt", [128, 2, D], F16))
            ps_o = sC.enter_context(nc.psum_tensor("ps_o", [128, 2, D], F32))

            k.barrier()
            t_wo = k.dma("gpsimd", lambda e: e.dma_start(
                out=wo16[:], in_=woTs.rearrange("(c p) n -> p c n", p=128)))
            k.wait_all("gpsimd", attn_dma_ticks)
            k.wait("tensor", t_wo)
            prev_pev = [None, None]
            prev_std = [None, None]
            prev_strip_mm = [None, None]
            rs_ticks = []
            ii2 = 0
            for o2 in range(8):
                for j4 in range(4):
                    tt = j4 * 8 + o2
                    pb = ii2 % 2
                    ii2 += 1
                    k.wait("sync", prev_strip_mm[pb])
                    if ii2 <= 2:
                        k.wait_all("sync", attn_dma_ticks)
                    t_as = k.dma("sync", lambda e, tt=tt, pb=pb: e.dma_start(
                        out=astr[:, :, pb, :],
                        in_=attn_d[:, tt * 128:(tt + 1) * 128].rearrange(
                            "(c p) t -> p c t", p=128)))
                    k.wait("tensor", prev_pev[pb])
                    k.wait("tensor", t_as)
                    fns = []
                    for sl in range(4):
                        for hc in range(4):
                            fns.append(lambda e, sl=sl, hc=hc, pb=pb:
                                       e.matmul(
                                           ps_o[:, pb, sl * 512:(sl + 1) * 512],
                                           astr[:, hc, pb, :],
                                           wo16[:, hc, sl * 512:(sl + 1) * 512],
                                           start=(hc == 0), stop=(hc == 3)))
                    t_mm = k.chain("tensor", fns)
                    prev_strip_mm[pb] = t_mm
                    k.wait("scalar", t_mm)
                    k.wait("scalar", prev_std[pb])
                    t_pe = k.op("scalar", lambda e, pb=pb: e.activation(
                        prt16[:, pb, :], ps_o[:, pb, :], AF.Copy, scale=1.0))
                    prev_pev[pb] = t_pe
                    k.wait("scalar", t_pe)
                    kch = o2 // 2
                    r0 = j4 * 256 + (o2 % 2) * 128
                    t_sd = k.dma("scalar", lambda e, r0=r0, pb=pb, kch=kch:
                                 e.dma_start(out=rs_in_c[kch][r0:r0 + 128, :],
                                             in_=prt16[:, pb, :]))
                    prev_std[pb] = t_sd
            k.wait("gpsimd", prev_std[0])
            k.wait("gpsimd", prev_std[1])
            for kch in range(4):
                if rs_ticks:
                    k.wait("gpsimd", rs_ticks[-1])
                t_rsk = k.cc(lambda e, kch=kch: e.collective_compute(
                    "ReduceScatter", ALU.add,
                    replica_groups=[[0, 1, 2, 3], [4, 5, 6, 7]],
                    ins=[rs_in_c[kch].opt()],
                    outs=[rs_out_c[kch].opt()]))
                rs_ticks.append(t_rsk)
            k.wait("gpsimd", rs_ticks[-1])
            t_bar_a = k.cc(lambda e: e.collective_compute(
                "AllReduce", ALU.add, replica_groups=[[0, 1, 2, 3, 4, 5, 6, 7]],
                ins=[asum_in.opt()], outs=[asum_out.opt()]))
            rs_gate = [t_bar_a, t_bar_a, t_bar_a, t_bar_a]

            prev_zt = [None, None]
            prev_use_o = [None, None]
            prev_use_x = [None, None]
            for tt in range(8):
                pb = tt % 2
                k.wait("sync", rs_gate[tt // 2])
                k.wait("sync", prev_use_o[pb])
                t_lo = k.dma("sync", lambda e, tt=tt, pb=pb: e.dma_start(
                    out=o16[:, pb, :],
                    in_=rs_out_c[tt // 2][(tt % 2) * 128:(tt % 2) * 128 + 128, :]))
                k.wait("scalar", prev_use_x[pb])
                t_lx = k.dma("scalar", lambda e, tt=tt, pb=pb: e.dma_start(
                    out=xot[:, pb, :], in_=x_own[tt * 128:(tt + 1) * 128, :]))
                k.wait("vector", t_lo)
                fns = []
                for q4 in range(4):
                    fns.append(lambda e, pb=pb, q4=q4: e.bn_stats(
                        bnst[:, q4 * 6:(q4 + 1) * 6],
                        o16[:, pb, q4 * 512:(q4 + 1) * 512]))
                t_bn = k.chain("vector", fns)
                k.wait("vector", t_bn)
                t_agg = k.op("vector", lambda e: e.bn_aggr(stat[:, 0:2], bnst[:]))
                k.wait("scalar", t_agg)
                t_rstd = k.chain("scalar", [
                    lambda e: e.activation(stat[:, 2:3], stat[:, 1:2],
                                           AF.Ln, bias=eps_col[:]),
                    lambda e: e.activation(stat[:, 3:4], stat[:, 2:3],
                                           AF.Exp, scale=-0.5),
                ])
                k.wait("vector", t_rstd)
                k.wait_all("vector", prev_use_o)
                t_mb0 = k.op("vector", lambda e: e.tensor_tensor(
                    stat[:, 16:17], stat[:, 0:1], stat[:, 3:4], ALU.mult))
                k.wait("vector", t_mb0)
                t_mb = k.op("vector", lambda e: e.tensor_scalar_mul(
                    stat[:, 16:17], stat[:, 16:17], -1.0))
                k.wait("scalar", t_mb)
                t_n = k.op("scalar", lambda e, pb=pb: e.activation(
                    tmp32[:], o16[:, pb, :], AF.Identity,
                    bias=stat[:, 16:17], scale=stat[:, 3:4]))
                prev_use_o[pb] = t_n
                k.wait("vector", t_lx)
                k.wait("vector", t_n)
                t_x1 = k.op("vector", lambda e, tt=tt, pb=pb: e.tensor_tensor(
                    x1[:, tt, :], tmp32[:], xot[:, pb, :], ALU.add))
                prev_use_x[pb] = t_x1
                k.wait("vector", t_x1)
                fns = []
                for q4 in range(4):
                    fns.append(lambda e, tt=tt, q4=q4: e.bn_stats(
                        bnst[:, q4 * 6:(q4 + 1) * 6],
                        x1[:, tt, q4 * 512:(q4 + 1) * 512]))
                t_bn2 = k.chain("vector", fns)
                k.wait("vector", t_bn2)
                t_agg2 = k.op("vector", lambda e: e.bn_aggr(stat[:, 4:6], bnst[:]))
                k.wait("scalar", t_agg2)
                t_rstd2 = k.chain("scalar", [
                    lambda e: e.activation(stat[:, 6:7], stat[:, 5:6],
                                           AF.Ln, bias=eps_col[:]),
                    lambda e: e.activation(stat[:, 7:8], stat[:, 6:7],
                                           AF.Exp, scale=-0.5),
                ])
                k.wait("vector", t_rstd2)
                k.wait_all("vector", [prev_zt[0], prev_zt[1]])
                t_mb2a = k.op("vector", lambda e: e.tensor_tensor(
                    stat[:, 17:18], stat[:, 4:5], stat[:, 7:8], ALU.mult))
                k.wait("vector", t_mb2a)
                t_mb2 = k.op("vector", lambda e: e.tensor_scalar_mul(
                    stat[:, 17:18], stat[:, 17:18], -1.0))
                k.wait("scalar", t_mb2)
                k.wait("scalar", prev_zt[pb])
                t_z = k.op("scalar", lambda e, tt=tt, pb=pb: e.activation(
                    zt[:, pb, :], x1[:, tt, :], AF.Identity,
                    bias=stat[:, 17:18], scale=stat[:, 7:8]))
                k.wait("sync", t_z)
                t_ztp = k.tdma("sync", lambda e, tt=tt, pb=pb: e.dma_start(
                    out=z_fm[:, :, tt * 128:(tt + 1) * 128],
                    in_=zt[:, pb, :], transpose=True))
                prev_zt[pb] = t_ztp
                last_zfm.append(t_ztp)
            if tp_x1 is not None:
                k.wait_all("sync", prev_use_x)
                k.dma("sync", lambda e: e.dma_start(out=tp_x1, in_=x1[:]))
            if tp_z is not None:
                k.wait_all("gpsimd", last_zfm)
                k.dma("gpsimd", lambda e: e.dma_start(out=tp_z, in_=z_fm[:]))
            k.flush()

        sD = ExitStack()
        with sD:
            h_fm = sD.enter_context(nc.sbuf_tensor("h_fm", [128, FC, 512], F16))
            t1s = sD.enter_context(nc.sbuf_tensor("t1s", [128, 3, 1024], F16))
            t2s = sD.enter_context(nc.sbuf_tensor("t2s", [128, 2, 2, D], F16))
            Mbuf = sD.enter_context(nc.sbuf_tensor("Mbuf", [128, 512], F32))
            Bbc = sD.enter_context(nc.sbuf_tensor("Bbc", [128, 512], F16))
            bpb = sD.enter_context(nc.sbuf_tensor("bpb", [128, 8], F32))
            vscr = sD.enter_context(nc.sbuf_tensor("vscr", [128, 4, 512], F16))
            of32 = sD.enter_context(nc.sbuf_tensor("of32", [128, 2, D], F32))
            ps_f = sD.enter_context(nc.psum_tensor("ps_f", [128, 8, 512], F32))

            k.barrier()
            t_out_ticks = []
            prev_ep_act = None
            k.wait("tensor", t_id)
            prev_f2_mm = None
            prev_ep = {}
            prev_out = [None, None]
            for hf in range(2):
                toff = hf * 512
                t_mset = k.op("vector", lambda e: e.memset(Mbuf[:], 0.0))
                k.wait("vector", t_mset)
                prev_t1ld = [None, None, None]
                prev_maxu = None
                blk_mm = {}
                h_ev = {}
                li = 0
                for hblk2 in range(8):
                    for dc in range(DC):
                        pp = li % 3
                        li += 1
                        if hblk2 == 0 and dc < 3:
                            k.wait("scalar", qstate["t1_done"])
                        k.wait("scalar", blk_mm.get((2 * hblk2 + 1, dc - 3)))
                        if dc < 3:
                            k.wait("scalar",
                                   blk_mm.get((2 * hblk2 - 1, DC - 3 + dc)))
                        t_ld = k.dma("scalar", lambda e, hblk2=hblk2, dc=dc,
                                     pp=pp: e.dma_start(
                                         out=t1s[:, pp, :],
                                         in_=t1c[dc * 128:(dc + 1) * 128,
                                                 hblk2 * 1024:
                                                 (hblk2 + 1) * 1024]))
                        prev_t1ld[pp] = t_ld
                        k.wait("tensor", t_ld)
                        if hblk2 == 0 and dc == 0:
                            k.wait_all("tensor", last_zfm)
                            if hf == 1:
                                k.wait("tensor", prev_f2_mm)
                        for half2 in range(2):
                            hblk = 2 * hblk2 + half2
                            sbase = (hblk % 2) * 4
                            if dc == 0:
                                for s4 in range(4):
                                    k.wait("tensor",
                                           h_ev.get((hblk - 2, s4)))
                            fns = []
                            for s4 in range(4):
                                fns.append(
                                    lambda e, dc=dc, pp=pp, s4=s4, toff=toff,
                                    sbase=sbase, half2=half2: e.matmul(
                                        ps_f[:, sbase + s4, :],
                                        t1s[:, pp, half2 * 512 + s4 * 128:
                                            half2 * 512 + (s4 + 1) * 128],
                                        z_fm[:, dc, toff:toff + 512],
                                        start=(dc == 0), stop=(dc == DC - 1)))
                            t_mm = k.chain("tensor", fns)
                            blk_mm[(hblk, dc)] = t_mm
                    for half2 in range(2):
                        hblk = 2 * hblk2 + half2
                        sbase = (hblk % 2) * 4
                        for s4 in range(4):
                            hc = hblk * 4 + s4
                            k.wait("scalar", blk_mm[(hblk, DC - 1)])
                            t_ev = k.op("scalar", lambda e, hc=hc, s4=s4,
                                        sbase=sbase: e.activation(
                                            h_fm[:, hc, :],
                                            ps_f[:, sbase + s4, :],
                                            AF.Relu, scale=rsbc[:, 0:1]))
                            h_ev[(hblk, s4)] = t_ev
                            k.wait("vector", t_ev)
                            k.wait("vector", prev_maxu)
                            t_mx = k.op("vector", lambda e, hc=hc:
                                        e.tensor_tensor(
                                            Mbuf[:], Mbuf[:],
                                            h_fm[:, hc, :], ALU.max))
                            prev_maxu = t_mx
                st_ticks = []
                last_mc = None
                for tl in range(4):
                    tg = hf * 4 + tl
                    k.wait("tensor", prev_maxu)
                    k.wait("tensor", last_mc)
                    t_tp = k.op("tensor", lambda e, tl=tl: e.transpose(
                        ps_f[:, 0, 0:128], Mbuf[:, tl * 128:(tl + 1) * 128],
                        id32[:]))
                    k.wait("vector", t_tp)
                    t_mc = k.op("vector", lambda e: e.reduce_max(
                        stat[:, 0:1], ps_f[:, 0, 0:128], axis=AX.X))
                    last_mc = t_mc
                    k.wait("vector", t_mc)
                    t_x2 = k.op("vector", lambda e: e.tensor_tensor(
                        stat[:, 1:2], stat[:, 0:1], stat[:, 0:1], ALU.mult))
                    k.wait("vector", t_x2)
                    t_xm = k.op("vector", lambda e: e.tensor_scalar_max(
                        stat[:, 1:2], stat[:, 1:2], EPS))
                    k.wait("vector", t_xm)
                    t_bi = k.op("vector", lambda e: e.reciprocal(
                        stat[:, 2:3], stat[:, 1:2]))
                    k.wait("vector", t_bi)
                    t_B = k.op("vector", lambda e, tl=tl: e.tensor_scalar_mul(
                        stat[:, 8 + tl:9 + tl], stat[:, 2:3], 255.0))
                    t_bp = k.op("vector", lambda e, tg=tg: e.tensor_scalar(
                        bpb[:, tg:tg + 1], stat[:, 1:2], rsbc[:, 1:2],
                        1.0 / 255.0, ALU.mult, op1=ALU.mult))
                    st_ticks.append(t_bp)
                    k.wait("tensor", t_B)
                    t_tb = k.op("tensor", lambda e, tl=tl: e.transpose(
                        ps_f[0:1, 1, tl * 128:(tl + 1) * 128],
                        stat[:, 8 + tl:9 + tl], id32[:]))
                    st_ticks.append(t_tb)
                k.wait_all("scalar", st_ticks)
                t_br = k.op("scalar", lambda e: e.activation(
                    of32[0:1, 0, 0:512], ps_f[0:1, 1, :], AF.Copy, scale=1.0))
                k.wait("tensor", t_br)
                t_bbc = k.op("tensor", lambda e: e.matmul(
                    ps_f[:, 2, :], ones_row[:], of32[0:1, 0, 0:512],
                    start=True, stop=True))
                k.wait("vector", t_bbc)
                t_bbce = k.op("vector", lambda e: e.tensor_copy(
                    Bbc[:], ps_f[:, 2, :]))
                prev_q = None
                qgrp = {}
                for g4 in range(FC // 4):
                    if g4 == 0:
                        k.wait("vector", t_bbce)
                    qa = []
                    for c4 in range(4):
                        hc = g4 * 4 + c4
                        t_qa = k.op("vector", lambda e, hc=hc, c4=c4:
                                    e.tensor_tensor(vscr[:, c4, :], Bbc[:],
                                                    h_fm[:, hc, :], ALU.mult))
                        qa.append(t_qa)
                    t_qm = None
                    for c4 in range(4):
                        hc = g4 * 4 + c4
                        k.wait("vector", qa[c4])
                        t_qm = k.op("vector", lambda e, hc=hc, c4=c4:
                                    e.tensor_tensor(h_fm[:, hc, :],
                                                    vscr[:, c4, :],
                                                    h_fm[:, hc, :], ALU.mult))
                    hv = h_fm[:, g4 * 4:(g4 + 1) * 4, :].rearrange(
                        "p a b -> p (a b)")
                    k.wait("vector", t_qm)
                    t_qc = k.op("vector", lambda e, hv=hv: e.tensor_scalar_add(
                        hv, hv, MAGIC))
                    k.wait("vector", t_qc)
                    t_q = k.op("vector", lambda e, hv=hv: e.tensor_scalar_sub(
                        hv, hv, MAGIC))
                    qgrp[g4] = t_q
                    prev_q = t_q
                if tp_h is not None and hf == 0:
                    k.wait("gpsimd", prev_q)
                    k.dma("gpsimd", lambda e: e.dma_start(out=tp_h, in_=h_fm[:]))
                if tp_bp is not None and hf == 1:
                    k.wait("sync", prev_q)
                    k.dma("sync", lambda e: e.dma_start(out=tp_bp, in_=bpb[:]))

                for sub in range(2):
                    prev_t2ld = [None, None]
                    last_mm_chunk = {}
                    for cp2 in range(FC // 2):
                        pp = cp2 % 2
                        if cp2 < 2:
                            k.wait("scalar", qstate["t2_done"])
                        k.wait("scalar", last_mm_chunk.get(cp2 - 2))
                        t_l2 = k.dma("scalar", lambda e, cp2=cp2, pp=pp:
                                     e.dma_start(
                                         out=t2s[:, pp, :, :],
                                         in_=t2c[cp2 * 256:(cp2 + 1) * 256, :]
                                         .rearrange("(a p) n -> p a n", p=128)))
                        prev_t2ld[pp] = t_l2
                        k.wait("tensor", t_l2)
                        k.wait("tensor", qgrp.get((2 * cp2 + 1) // 4))
                        if cp2 == 0:
                            k.wait("tensor", t_bbce)
                            k.wait_all("tensor", list(prev_ep.values()))
                        fns = []
                        for ci in range(2):
                            c = cp2 * 2 + ci
                            for tloc in range(2):
                                for sl in range(4):
                                    slot = tloc * 4 + sl
                                    ts_ = (sub * 2 + tloc) * 128
                                    fns.append(
                                        lambda e, c=c, ci=ci, pp=pp, slot=slot,
                                        ts_=ts_, sl=sl: e.matmul(
                                            ps_f[:, slot, :],
                                            h_fm[:, c, ts_:ts_ + 128],
                                            t2s[:, pp, ci,
                                                sl * 512:(sl + 1) * 512],
                                            start=(c == 0), stop=(c == FC - 1)))
                        t_mm = k.chain("tensor", fns)
                        last_mm_chunk[cp2] = t_mm
                        prev_f2_mm = t_mm
                    for tloc in range(2):
                        tg = hf * 4 + sub * 2 + tloc
                        pb = tloc
                        k.wait("scalar", prev_f2_mm)
                        k.wait("scalar", prev_out[pb])
                        t_a1 = k.op("scalar", lambda e, tloc=tloc, tg=tg, pb=pb:
                                    e.activation(
                                        of32[:, pb, :],
                                        ps_f[:, tloc * 4:(tloc + 1) * 4, :]
                                        .rearrange("p a b -> p (a b)"),
                                        AF.Identity, scale=bpb[:, tg:tg + 1]))
                        k.wait("vector", t_a1)
                        fns = []
                        for q4 in range(4):
                            fns.append(lambda e, pb=pb, q4=q4: e.bn_stats(
                                bnst[:, q4 * 6:(q4 + 1) * 6],
                                of32[:, pb, q4 * 512:(q4 + 1) * 512]))
                        t_bn = k.chain("vector", fns)
                        k.wait("vector", t_bn)
                        t_agg = k.op("vector", lambda e: e.bn_aggr(stat[:, 0:2], bnst[:]))
                        k.wait("scalar", t_agg)
                        t_rstd = k.chain("scalar", [
                            lambda e: e.activation(stat[:, 2:3], stat[:, 1:2],
                                                   AF.Ln, bias=eps_col[:]),
                            lambda e: e.activation(stat[:, 3:4], stat[:, 2:3],
                                                   AF.Exp, scale=-0.5),
                        ])
                        k.wait("vector", t_rstd)
                        k.wait("vector", prev_ep_act)
                        t_mb3a = k.op("vector", lambda e: e.tensor_tensor(
                            stat[:, 16:17], stat[:, 0:1], stat[:, 3:4],
                            ALU.mult))
                        k.wait("vector", t_mb3a)
                        t_mb3 = k.op("vector", lambda e: e.tensor_scalar_mul(
                            stat[:, 16:17], stat[:, 16:17], -1.0))
                        k.wait("scalar", t_mb3)
                        t_n = k.op("scalar", lambda e, pb=pb: e.activation(
                            of32[:, pb, :], of32[:, pb, :], AF.Identity,
                            bias=stat[:, 16:17], scale=stat[:, 3:4]))
                        prev_ep_act = t_n
                        k.wait("vector", t_n)
                        t_fin = k.op("vector", lambda e, tg=tg, pb=pb:
                                     e.tensor_tensor(of32[:, pb, :],
                                                     of32[:, pb, :],
                                                     x1[:, tg, :], ALU.add))
                        for sl4 in range(4):
                            prev_ep[tloc * 4 + sl4] = t_fin
                        k.wait("sync", t_fin)
                        t_o = k.dma("sync", lambda e, tg=tg, pb=pb: e.dma_start(
                            out=out[tg * 128:(tg + 1) * 128, :],
                            in_=of32[:, pb, :]))
                        t_out_ticks.append(t_o)
                        prev_out[pb] = t_o
            if tp_rsi is not None:
                k.barrier()
                for kk2 in range(4):
                    k.dma("gpsimd", lambda e, kk2=kk2: e.dma_start(
                        out=tp_rsi[kk2], in_=rs_in_c[kk2][:, :]))
                    k.dma("gpsimd", lambda e, kk2=kk2: e.dma_start(
                        out=tp_rso[kk2], in_=rs_out_c[kk2][:, :]))
            if tp_t1 is not None:
                k.wait("gpsimd", qstate["t1_done"])
                k.dma("gpsimd", lambda e: e.dma_start(out=tp_t1, in_=t1c[:, :]))
            if tp_t2 is not None:
                k.wait("gpsimd", qstate["t2_done"])
                k.dma("gpsimd", lambda e: e.dma_start(out=tp_t2, in_=t2c[:, :]))
            k.wait_all("sync", t_out_ticks)
            if any(t is not None for t in
                   (tp_e_z, tp_e_x1, tp_e_h, tp_e_bp, tp_e_bbc)):
                k.barrier()
                for tp, src_ap in ((tp_e_z, z_fm), (tp_e_x1, x1),
                                   (tp_e_h, h_fm), (tp_e_bp, bpb),
                                   (tp_e_bbc, Bbc)):
                    if tp is not None:
                        k.dma("gpsimd", lambda e, tp=tp, s=src_ap:
                              e.dma_start(out=tp, in_=s[:]))
                k.wait("sync", (("d", "gpsimd"), k.cnt[("d", "gpsimd")]))
            k.flush()

    return nc


def kernel(L0_attn_hidden_states, L0_attn_w_qkv, L0_attn_w_o,
           L0_ffn_c_fc, L0_ffn_c_proj, _taps=()):
    x = np.ascontiguousarray(np.asarray(L0_attn_hidden_states, dtype=np.float32))
    wqkv = np.asarray(L0_attn_w_qkv, dtype=np.float32)
    wo = np.asarray(L0_attn_w_o, dtype=np.float32)
    cfcv = np.ascontiguousarray(np.asarray(L0_ffn_c_fc, dtype=np.float32))
    cp = np.asarray(L0_ffn_c_proj, dtype=np.float32)

    cpTv = np.ascontiguousarray(cp.T)
    woT = np.ascontiguousarray(wo.T)
    ident = np.eye(128, dtype=np.float32)
    ii = np.arange(128)
    mask = np.where(ii[None, :] <= ii[:, None], 0.0, NEG).astype(np.float32)

    in_maps = []
    for c in range(8):
        b, j = divmod(c, 4)
        in_maps.append(dict(
            x_full=np.ascontiguousarray(x[b]),
            x_own=np.ascontiguousarray(x[b, TC * j:TC * (j + 1)]),
            wqT=np.ascontiguousarray(wqkv[512 * j:512 * (j + 1)].T),
            wkT=np.ascontiguousarray(wqkv[D + 512 * j:D + 512 * (j + 1)].T),
            wvT=np.ascontiguousarray(
                wqkv[2 * D + 512 * j:2 * D + 512 * (j + 1)].T),
            woTs=np.ascontiguousarray(woT[512 * j:512 * (j + 1)]),
            cfc=cfcv,
            cpT=cpTv,
            cfc_sl=np.ascontiguousarray(cfcv[256 * c:256 * (c + 1)]),
            cpT_sl=np.ascontiguousarray(cpTv[1024 * c:1024 * (c + 1)]),
            identD=ident,
            maskD=mask,
        ))

    nc = build_program(taps=_taps)
    res = run_bass_kernel_spmd(nc, in_maps, core_ids=list(range(8)))

    outv = np.empty((B, S, D), dtype=np.float32)
    for c in range(8):
        b, j = divmod(c, 4)
        outv[b, TC * j:TC * (j + 1)] = res.results[c]["out"]
    if _taps:
        kernel._tap_results = [
            {kk2: vv for kk2, vv in r.items() if kk2.startswith("tap_")}
            for r in res.results]
    return outv
